# version 02
# baseline (speedup 1.0000x reference)
import math
from contextlib import ExitStack

import numpy as np
import ml_dtypes
import concourse.bass as bass
import concourse.mybir as mybir

F32 = mybir.dt.float32
BF16 = mybir.dt.bfloat16
FP8 = mybir.dt.float8e4
AF = mybir.ActivationFunctionType
ALU = mybir.AluOpType
AX = mybir.AxisListType.X

NCORES = 8
TOK = 1024
TT = 8
D = 2048
DT = 16
H = 8
HD = 256
CH = 4
CS = 256
DI = 6144
SCALE = 16.0
EPS = 1e-5


class Sem:
    def __init__(self, h):
        self.h = h
        self.n = 0


class Evt:
    __slots__ = ("sem", "n")
    def __init__(self, sem=None, n=0):
        self.sem = sem
        self.n = n


class B:
    ENGINES = ("sync", "scalar", "vector", "tensor", "gpsimd")

    def __init__(self, nc, stack):
        self.nc = nc
        self.stack = stack
        self.ops = {e: [] for e in self.ENGINES}

    def sem(self, name):
        return Sem(self.stack.enter_context(self.nc.semaphore(name)))

    def emit(self, eng, fn):
        self.ops[eng].append(fn)

    def wait(self, eng, sem, val=None):
        v = sem.n if val is None else val
        assert v >= 0
        self.emit(eng, lambda e, s=sem.h, v=v: e.wait_ge(s, v))

    def waitev(self, eng, evt):
        if evt is not None and evt.sem is not None and evt.n > 0:
            self.wait(eng, evt.sem, evt.n)

    def dma(self, eng, out, in_, sem, **kw):
        sem.n += 16
        self.emit(eng, lambda e, o=out, i=in_, s=sem.h, kw=kw:
                  e.dma_start(out=o, in_=i, **kw).then_inc(s, 16))
        return Evt(sem, sem.n)

    def op(self, eng, meth, *args, sem=None, inc=1, **kw):
        if sem is not None:
            sem.n += inc
        def f(e, m=meth, a=args, kw=kw, s=sem, i=inc):
            inst = getattr(e, m)(*a, **kw)
            if s is not None:
                inst.then_inc(s.h, i)
        self.emit(eng, f)
        return Evt(sem, sem.n) if sem is not None else None


def build_program(debug=(), last_phase="p5", p1_heads=H, p1_mode="full"):
    stack = ExitStack()
    nc = bass.Bass("TRN2", target_bir_lowering=False, debug=False)
    b = B(nc, stack)

    def inp(name, shape, dt=F32):
        return nc.dram_tensor(name, list(shape), dt, kind="ExternalInput").ap()

    x_in = inp("x", [TOK, D])
    wq = inp("wq", [D, D]); wk = inp("wk", [D, D]); wv = inp("wv", [D, D])
    wg = inp("wg", [D, D]); wo = inp("wo", [D, D])
    gate_w = inp("gate_w", [D, DI]); up_w = inp("up_w", [D, DI])
    down_w = inp("down_w", [DI, D])
    biasf = inp("biasf", [3, D])
    bcols = inp("bcols", [128, 2 * DT])
    gncols_in = inp("gncols", [128, 2 * DT])
    sccols_in = inp("sccols", [128, 96 + DT])
    sin_in = inp("sin", [TOK, 128]); cos_in = inp("cos", [TOK, 128])
    mask_in = inp("maskT", [H, 2 * 128, CS])
    kdec_in = inp("kdec", [128, 2 * H])
    qdec_in = inp("qdec", [128, 2 * H])
    coef_in = inp("coef", [128, H * NCORES])
    ident_in = inp("ident_t", [128, 128])

    y_out = nc.dram_tensor("y", [TOK, D], F32, kind="ExternalOutput").ap()

    cc_in = nc.dram_tensor("cc_in", [H * 2 * 128, HD], F32)
    cc_out = nc.dram_tensor("cc_out", [NCORES, H * 2 * 128, HD], F32,
                            addr_space="Shared")
    g8_dram = nc.dram_tensor("g8_dram", [D, DI], FP8)
    u8_dram = nc.dram_tensor("u8_dram", [D, DI], FP8)
    d8_dram = nc.dram_tensor("d8_dram", [DI, D], FP8)
    g8d3 = g8_dram.ap().rearrange("(d p) c -> p d c", p=128)
    u8d3 = u8_dram.ap().rearrange("(d p) c -> p d c", p=128)
    d8d3 = d8_dram.ap().rearrange("(d p) c -> p d c", p=128)

    dbg = {}
    def dbgout(name, shape):
        if name in debug:
            dbg[name] = nc.dram_tensor("dbg_" + name, list(shape), F32,
                                       kind="ExternalOutput").ap()
        return dbg.get(name)

    sbt = lambda name, shape, dt: stack.enter_context(
        nc.sbuf_tensor(name, list(shape), dt)).ap()
    psf = lambda name: stack.enter_context(
        nc.psum_tensor(name, [128, 512], F32)).ap()

    xn_fm = sbt("xn_fm", [128, DT * TOK], BF16)
    k_fm = sbt("k_fm", [128, DT * TOK], BF16)
    s_fm = sbt("s_fm", [128, DT * TOK], BF16)
    v_tm = sbt("v_tm", [128, TT * D], BF16)
    rhat_tm = sbt("rhat_tm", [128, TT * D], BF16)
    poolE = sbt("poolE", [128, 16 * 1024], BF16)
    regH = sbt("regH", [128, 4 * 1024], BF16)
    wbufA = sbt("wbufA", [128, DT * 128], BF16)
    gn_sb = sbt("gn_sb", [128, 2 * DT], F32)
    bc_sb = sbt("bc_sb", [128, 2 * DT], F32)
    sc_sb = sbt("sc_sb", [128, 96 + DT], F32)
    kdec_sb = sbt("kdec_sb", [128, 2 * H], F32)
    qdec_sb = sbt("qdec_sb", [128, 2 * H], F32)
    coef_sb = sbt("coef_sb", [128, H * NCORES], F32)
    ident = sbt("ident", [128, 128], BF16)
    ST = sbt("ST", [128, 64], F32)

    sfmf = s_fm.bitcast(F32)
    ktmp = s_fm[:, 0:512]
    rtmp = s_fm[:, 512:768]
    krot_s = s_fm[:, 768:1792]
    simT_sb = s_fm[:, 1792:2304]
    F_sb = s_fm[:, 2304:2816]
    Ft_sb = s_fm[:, 2816:3328]
    ret_sc = sfmf[:, 1664:2176]
    int_sc = sfmf[:, 2176:2688]
    p3sq = sfmf[:, 2688:2944]
    Lcur = sfmf[:, 2944:3456]
    kd_sb = s_fm[:, 6912:7424]
    sin_sb = sfmf[:, 3712:4736]
    cos_sb = sfmf[:, 4736:5760]
    wbufB = s_fm[:, 11520:13568]
    mask_sb = sfmf[:, 6784:7296]
    agbuf2 = [sfmf[:, 2944:3456], sfmf[:, 7680:8192]]
    agt_sb = s_fm[:, 14848:15360]
    bb_q = s_fm[:, 14592:14848]
    bb_k = s_fm[:, 14848:15104]
    bb_v = s_fm[:, 15104:15360]
    rn_bf = k_fm[:, 0:1024]
    tg_sb = k_fm[:, 1024:2048]
    tu_sb = k_fm[:, 2048:2560]
    wbufO = k_fm[:, 2560:4608]
    rhf = rhat_tm.bitcast(F32)
    xrow = rhf[:, 0:4096]
    sq_sc = rhf[:, 4096:6144]
    r8 = rhat_tm.bitcast(FP8)
    xn_scr = v_tm[:, 0:8192]
    out_sb = v_tm.bitcast(F32)[:, 0:4096]

    def L_ap(h, j, pt):
        off = ((h * 3 + (j - 1)) * 2 + pt) * HD
        return poolE[:, off:off + HD]
    def Sin_ap(h, pt):
        off = 12 * 1024 + (h * 2 + pt) * HD
        return poolE[:, off:off + HD]
    def gate_ap(d):
        return poolE[:, d * TOK:(d + 1) * TOK]
    def fuse_ap(i):
        return poolE[:, i * 256:(i + 1) * 256]

    psA, psB, psC, psD_, psE, psF, psT1, psT2 = [psf(f"ps{i}") for i in range(8)]
    psT1b = psT1.bitcast(BF16)
    psT2b = psT2.bitcast(BF16)
    allps = [psA, psB, psC, psD_, psE, psF, psT1, psT2]

    sW = b.sem("sW"); sHW = b.sem("sHW")
    sT = b.sem("sT"); sV = b.sem("sV"); sS = b.sem("sS")
    sCC = b.sem("sCC"); sST = b.sem("sST"); sBAR = b.sem("sBAR")
    sG = b.sem("sG")
    sPRE = b.sem("sPRE")
    sX = [b.sem("sX0"), b.sem("sX1")]
    sWA = b.sem("sWA"); sWB = b.sem("sWB")
    sBQ = b.sem("sBQ"); sBK = b.sem("sBK"); sBV = b.sem("sBV")
    sMSK = b.sem("sMSK"); sCCI = b.sem("sCCI")
    sSTp = [b.sem("sSTa"), b.sem("sSTb")]
    sAG = [b.sem("sAG0"), b.sem("sAG1")]
    sG8 = [b.sem("sG8a"), b.sem("sG8b")]
    sU8 = [b.sem("sU8a"), b.sem("sU8b")]
    sD8 = [b.sem("sD8a"), b.sem("sD8b")]
    sG8s = [b.sem("sG8sa"), b.sem("sG8sb")]
    sU8s = [b.sem("sU8sa"), b.sem("sU8sb")]
    sD8s = [b.sem("sD8sa"), b.sem("sD8sb")]
    sRSl = [b.sem(f"sRSl{i}") for i in range(4)]
    sRSw = [b.sem(f"sRSw{i}") for i in range(4)]
    sRW = [b.sem("sRW0"), b.sem("sRW1")]
    sIDE = b.sem("sIDE")
    sDBG = b.sem("sDBG")

    def finish():
        b.wait("sync", sST)
        b.wait("sync", sSTp[0])
        b.wait("sync", sSTp[1])
        b.wait("sync", sDBG)
        with nc.Block() as block:
            for eng in B.ENGINES:
                def run(e, eng=eng):
                    for f in b.ops[eng]:
                        f(e)
                getattr(block, eng)(run)
        return nc, stack, sorted(dbg.keys())

    def barrier():
        for e in B.ENGINES:
            sBAR.n += 1
            b.emit(e, lambda eng, s=sBAR.h: eng.sem_inc(s, 1))
        for e in B.ENGINES:
            b.wait(e, sBAR)

    def semtick(eng, sem):
        sem.n += 1
        b.emit(eng, lambda e, s=sem.h: e.sem_inc(s, 1))
        return Evt(sem, sem.n)

    def hazard_gap():
        ev = semtick("vector", sV)
        b.wait("scalar", sV, ev.n)
        es = semtick("scalar", sS)
        b.wait("vector", sS, es.n)

    w3 = {n: t.rearrange("(d p) c -> p d c", p=128) for n, t in
          [("wq", wq), ("wk", wk), ("wv", wv), ("wg", wg), ("wo", wo),
           ("gate_w", gate_w), ("up_w", up_w), ("down_w", down_w)]}
    wbufA3 = wbufA.rearrange("p (d c) -> p d c", c=128)
    wbufB3 = wbufB.rearrange("p (d c) -> p d c", c=128)

    _s8 = s_fm.bitcast(FP8)
    rs_slots = [_s8[:, off * 2:(off + 1024) * 2]
                for off in (1792, 2816, 3840, 13568)]
    rs_jobs = []
    for _g in range(48):
        rs_jobs.append(("gu16", w3["gate_w"][:, :, _g * 128:(_g + 1) * 128],
                        g8d3[:, :, _g * 128:(_g + 1) * 128]))
    for _g in range(48):
        rs_jobs.append(("gu16", w3["up_w"][:, :, _g * 128:(_g + 1) * 128],
                        u8d3[:, :, _g * 128:(_g + 1) * 128]))
    for _d in range(48):
        rs_jobs.append(("d1", w3["down_w"][:, _d:_d + 1, :],
                        d8d3[:, _d:_d + 1, :]))
    rs_next = [0]
    rs_wb = [Evt(), Evt(), Evt(), Evt()]

    def restage_step(n):
        for _ in range(n):
            i = rs_next[0]
            if i >= len(rs_jobs):
                return
            kind, srcap, dstap = rs_jobs[i]
            slot = i % 4
            view = rs_slots[slot].rearrange(
                "p (d c) -> p d c", c=(128 if kind == "gu16" else 2048))
            b.waitev("gpsimd", rs_wb[slot])
            ld = b.dma("gpsimd", view, srcap, sRSl[slot])
            b.waitev("sync", ld)
            rs_wb[slot] = b.dma("sync", dstap, view, sRSw[slot])
            rs_next[0] += 1



    b.dma("sync", sin_sb.rearrange("p (t c) -> p t c", c=128),
          sin_in.rearrange("(t p) c -> p t c", p=128), sPRE)
    b.dma("sync", cos_sb.rearrange("p (t c) -> p t c", c=128),
          cos_in.rearrange("(t p) c -> p t c", p=128), sPRE)
    b.dma("sync", gn_sb[:, :], gncols_in, sPRE)
    b.dma("sync", sc_sb[:, :], sccols_in, sPRE)
    b.dma("sync", kdec_sb[:, :], kdec_in, sPRE)
    b.dma("sync", qdec_sb[:, :], qdec_in, sPRE)
    b.dma("sync", coef_sb[:, :], coef_in, sPRE)
    b.dma("sync", bc_sb[:, :], bcols, sPRE)
    ident_ev = b.dma("gpsimd", ident[:, :], ident_in, sIDE)
    tables_ev = Evt(sPRE, sPRE.n)

    g_ = 1.0 - np.exp(np.linspace(math.log(1 / 32), math.log(1 / 512), H))
    g256 = [float(g_[h] ** 256) for h in range(H)]
    gpow = [[float(g_[h] ** (256.0 * j)) for j in range(CH)] for h in range(H)]

    psT_free = [Evt(), Evt()]

    def proj_half(pst, wbuf3, t):
        for dt in range(DT):
            b.op("tensor", "matmul", pst,
                 xn_fm[:, dt * TOK + t * 128:dt * TOK + (t + 1) * 128],
                 wbuf3[:, dt, :], start=(dt == 0), stop=(dt == DT - 1),
                 sem=(sT if dt == DT - 1 else None))
        return Evt(sT, sT.n)

    rope_gfree = [Evt()]

    def rope6(dst, src, t):
        cs = cos_sb[:, t * 128:(t + 1) * 128]
        sn = sin_sb[:, t * 128:(t + 1) * 128]
        de, do = dst[:, 0:256:2], dst[:, 1:256:2]
        se, so = src[:, 0:256:2], src[:, 1:256:2]
        tm1 = rtmp[:, 0:128]
        tm2 = rtmp[:, 128:256]
        b.waitev("vector", rope_gfree[0])
        b.op("vector", "tensor_mul", de, se, cs)
        b.op("vector", "tensor_mul", do, so, cs)
        b.op("vector", "tensor_mul", tm1, so, sn)
        ev = b.op("vector", "tensor_mul", tm2, se, sn, sem=sV)
        b.wait("gpsimd", sV, ev.n)
        b.op("gpsimd", "tensor_sub", de, de, tm1)
        rope_gfree[0] = b.op("gpsimd", "tensor_add", do, do, tm2, sem=sG)
        return rope_gfree[0]

    xrow_v = lambda par: xrow[:, par * D:(par + 1) * D]
    xnscr_v = lambda par: xn_scr[:, par * D:(par + 1) * D]
    norm_done = [None] * TT
    xload = [None] * TT
    stat_done = [None] * TT

    b.waitev("tensor", ident_ev)
    b.waitev("vector", tables_ev)

    red_done = [None] * TT
    for t in range(TT):
        if t >= 2:
            b.waitev("sync", stat_done[t - 2])
            b.waitev("sync", red_done[t - 2])
        xload[t] = b.dma("sync", xrow_v(t % 2),
                         x_in[t * 128:(t + 1) * 128, :], sX[t % 2])
        par = t % 2
        b.waitev("vector", xload[t])
        red_done[t] = b.op("vector", "reduce_sum", ST[:, t:t + 1],
                           xrow_v(par), AX, sem=sV)
        b.waitev("scalar", xload[t])
        b.wait("scalar", sS)
        stat_done[t] = b.op("scalar", "activation", sq_sc[:, :], xrow_v(par),
                            AF.Square, accum_out=ST[:, 8 + t:9 + t], sem=sS)
    b.wait("vector", sS)
    b.wait("vector", sV)
    b.op("vector", "tensor_scalar_mul", ST[:, 16:24], ST[:, 0:8], 1.0 / D)
    ev = b.op("vector", "tensor_scalar", ST[:, 24:32], ST[:, 8:16],
              1.0 / D, EPS, ALU.mult, ALU.add, sem=sV)
    b.wait("scalar", sV, ev.n)
    ev = b.op("scalar", "activation", ST[:, 32:40], ST[:, 16:24],
              AF.Square, sem=sS)
    b.wait("vector", sS, ev.n)
    ev = b.op("vector", "tensor_sub", ST[:, 40:48], ST[:, 24:32],
              ST[:, 32:40], sem=sV)
    b.wait("scalar", sV, ev.n)
    ev = b.op("scalar", "activation", ST[:, 48:56], ST[:, 40:48],
              AF.Sqrt, sem=sS)
    b.wait("vector", sS, ev.n)
    b.op("vector", "reciprocal", ST[:, 56:64], ST[:, 48:56], sem=sV)
    hazard_gap()
    hazard_gap()
    psT_free = [Evt(), Evt()]

    xscr_free = [Evt(), Evt()]
    for t in range(TT):
        par = t % 2
        if t >= 2:
            b.waitev("sync", norm_done[t - 2])
        else:
            b.waitev("sync", stat_done[TT - 2 + t])
            b.waitev("sync", red_done[TT - 2 + t])
        xload[t] = b.dma("sync", xrow_v(par),
                         x_in[t * 128:(t + 1) * 128, :], sX[par])
        b.waitev("vector", xload[t])
        if t >= 2:
            b.waitev("vector", xscr_free[par])
        norm_done[t] = b.op("vector", "tensor_scalar", xnscr_v(par),
                            xrow_v(par), ST[:, 16 + t:17 + t],
                            ST[:, 56 + t:57 + t],
                            ALU.subtract, ALU.mult, sem=sV)
        b.waitev("tensor", norm_done[t])
        for half in range(2):
            pst = psT1b if half == 0 else psT2b
            b.waitev("tensor", psT_free[half])
            for dd in range(8):
                d = half * 8 + dd
                b.op("tensor", "transpose", pst[:, dd * 128:(dd + 1) * 128],
                     xnscr_v(par)[:, d * 128:(d + 1) * 128], ident[:, :],
                     sem=(sT if dd == 7 else None))
            b.wait("scalar", sT)
            srcp = pst.rearrange("p (d k) -> p d k", k=128)
            dstp = xn_fm.rearrange("p (d k) -> p d k", k=TOK)[
                :, half * 8:half * 8 + 8, t * 128:(t + 1) * 128]
            psT_free[half] = b.op("scalar", "copy", dstp, srcp, sem=sS)
        xscr_free[par] = Evt(sT, sT.n)

    if dbgout("xn_fm", [128, DT * TOK]):
        b.wait("gpsimd", sS)
        b.dma("gpsimd", dbg["xn_fm"], xn_fm[:, :], sDBG)
        b.wait("gpsimd", sDBG)
    if dbgout("ST", [128, 64]):
        b.wait("sync", sV)
        b.dma("sync", dbg["ST"], ST[:, :], sDBG)
        b.wait("sync", sDBG)
    if dbgout("xn_tm", [128, 2 * D]):
        b.wait("gpsimd", sV)
        b.dma("gpsimd", dbg["xn_tm"], xn_scr[:, 0:2 * D], sDBG)
        b.wait("gpsimd", sDBG)

    if last_phase == "p0":
        return finish()
    if p1_mode != "nobar":
        barrier()
    psT_free = [Evt(), Evt()]

    def head_dma(wname, h, ev_gate):
        b.waitev("gpsimd", ev_gate)
        ea = b.dma("gpsimd", wbufA3, w3[wname][:, :, h * 256:h * 256 + 128], sWA)
        eb = b.dma("gpsimd", wbufB3,
                   w3[wname][:, :, h * 256 + 128:h * 256 + 256], sWB)
        return (ea, eb)

    k_evac = [Evt()] * TT
    v_evac = [Evt()] * TT
    rope_done = [Evt()] * TT
    transp_done = [Evt()] * H
    psC_free = [Evt()]
    dS_done = [Evt()]
    L_copy = [Evt()]
    L_dma = [Evt()]
    cc_dma_evts = []

    bias_sems = {0: sBQ, 1: sBK, 2: sBV}

    def bias_bcast(dst, bi, h, prev_use):
        b.waitev("gpsimd", prev_use)
        return b.dma("gpsimd", dst,
                     biasf[bi:bi + 1, h * 256:(h + 1) * 256].broadcast_to(
                         (128, 256)), bias_sems[bi])

    restage_step(8)
    kdma = (Evt(), Evt())
    if p1_mode != "none":
        kdma = head_dma("wk", 0, ident_ev)
    for h in range(0 if p1_mode == "none" else p1_heads):
        bkev = bias_bcast(bb_k, 1, h, k_evac[TT - 1])
        bvev = bias_bcast(bb_v, 2, h, v_evac[TT - 1])
        if p1_mode == "dma":
            continue
        b.waitev("tensor", kdma[0])
        b.waitev("tensor", kdma[1])
        for t in range(TT):
            pst = (psA if t % 2 == 0 else psB)[:, 0:256]
            b.waitev("tensor", k_evac[t - 2] if t >= 2 else k_evac[TT - 2 + t])
            proj_half(pst[:, 0:128], wbufA3, t)
            proj_half(pst[:, 128:256], wbufB3, t)
            if p1_mode == "mm":
                k_evac[t] = Evt(sT, sT.n)
                rope_done[t] = k_evac[t]
                continue
            b.wait("vector", sT)
            if t == 0:
                b.waitev("vector", bkev)
            bv = b.op("vector", "tensor_add", pst, pst, bb_k, sem=sV)
            if p1_mode == "mmb":
                k_evac[t] = bv
                rope_done[t] = bv
                continue
            b.wait("scalar", sV, bv.n)
            k_evac[t] = b.op("scalar", "copy",
                             ktmp[:, (t % 2) * 256:(t % 2) * 256 + 256],
                             pst, sem=sS)
            if p1_mode == "noro":
                rope_done[t] = k_evac[t]
                continue
            b.waitev("vector", k_evac[t])
            if t == 0 and h > 0:
                b.waitev("vector", transp_done[h - 1])
            rope_done[t] = rope6(regH[:, t * 256:(t + 1) * 256],
                                 ktmp[:, (t % 2) * 256:(t % 2) * 256 + 256], t)
        if p1_mode == "k":
            continue
        vdma = head_dma("wv", h, Evt(sT, sT.n))
        b.waitev("tensor", vdma[0])
        b.waitev("tensor", vdma[1])
        for t in range(TT):
            pst = (psD_ if t % 2 == 0 else psE)[:, 0:256]
            b.waitev("tensor", v_evac[t - 2] if t >= 2 else v_evac[TT - 2 + t])
            proj_half(pst[:, 0:128], wbufA3, t)
            proj_half(pst[:, 128:256], wbufB3, t)
            b.wait("vector", sT)
            if t == 0:
                b.waitev("vector", bvev)
            v_evac[t] = b.op("vector", "tensor_add",
                             v_tm[:, t * D + h * 256:t * D + (h + 1) * 256],
                             pst, bb_v, sem=sV)
        if h + 1 < H:
            kdma = head_dma("wk", h + 1, Evt(sT, sT.n))
        if p1_mode == "kv":
            continue
        for half in range(2):
            pst = psT1b if half == 0 else psT2b
            b.waitev("tensor", rope_done[half * 4 + 3])
            b.waitev("tensor", psT_free[half])
            slot = 0
            for pt in range(2):
                for tl in range(4):
                    t = half * 4 + tl
                    b.op("tensor", "transpose",
                         pst[:, slot * 128:(slot + 1) * 128],
                         regH[:, t * 256 + pt * 128:t * 256 + pt * 128 + 128],
                         ident[:, :], sem=(sT if slot == 7 else None))
                    slot += 1
            b.wait("scalar", sT)
            src = pst.rearrange("p (a b k) -> p a b k", a=2, k=128)
            dst = k_fm.rearrange("p (a k) -> p a k", k=TOK)[
                :, 2 * h:2 * h + 2,
                (half * 4) * 128:(half * 4 + 4) * 128].rearrange(
                "p a (b k) -> p a b k", k=128)
            psT_free[half] = b.op("scalar", "copy", dst, src, sem=sS)
        transp_done[h] = Evt(sT, sT.n)
        if p1_mode == "kvt":
            continue
        b.waitev("vector", L_dma[0])
        b.op("vector", "memset", Lcur[:, :], 0.0)
        for j in range(CH):
            b.waitev("vector", L_copy[0])
            b.waitev("vector", dS_done[0])
            kd_ev = None
            for kt in range(2):
                kd_ev = b.op("vector", "tensor_scalar_mul",
                             kd_sb[:, kt * 256:(kt + 1) * 256],
                             regH[:, (2 * j + kt) * 256:(2 * j + kt + 1) * 256],
                             kdec_sb[:, h * 2 + kt:h * 2 + kt + 1], sem=sV)
            b.waitev("tensor", kd_ev)
            b.waitev("tensor", psC_free[0])
            for dkh in range(2):
                for kt in range(2):
                    b.op("tensor", "matmul", psC[:, dkh * 256:(dkh + 1) * 256],
                         kd_sb[:, kt * 256 + dkh * 128:
                               kt * 256 + dkh * 128 + 128],
                         v_tm[:, (2 * j + kt) * D + h * 256:
                              (2 * j + kt) * D + (h + 1) * 256],
                         start=(kt == 0), stop=(kt == 1),
                         sem=(sT if (dkh == 1 and kt == 1) else None))
            dS_done[0] = Evt(sT, sT.n)
            b.wait("vector", sT)
            for dkh in range(2):
                psC_free[0] = b.op("vector", "scalar_tensor_tensor",
                                   Lcur[:, dkh * 256:(dkh + 1) * 256],
                                   Lcur[:, dkh * 256:(dkh + 1) * 256], g256[h],
                                   psC[:, dkh * 256:(dkh + 1) * 256],
                                   ALU.mult, ALU.add, sem=sV)
            if j < CH - 1:
                b.waitev("scalar", psC_free[0])
                for pt in range(2):
                    L_copy[0] = b.op("scalar", "copy", L_ap(h, j + 1, pt),
                                     Lcur[:, pt * 256:(pt + 1) * 256], sem=sS)
            else:
                b.waitev("sync", psC_free[0])
                for pt in range(2):
                    L_dma[0] = b.dma(
                        "sync", cc_in.ap()[(h * 2 + pt) * 128:
                                           (h * 2 + pt + 1) * 128, :],
                        Lcur[:, pt * 256:(pt + 1) * 256], sCCI)
                cc_dma_evts.append(L_dma[0])
        restage_step(17)

    restage_step(len(rs_jobs))
    if dbgout("k_fm", [128, DT * TOK]):
        b.wait("gpsimd", sS)
        b.wait("gpsimd", sV)
        b.dma("gpsimd", dbg["k_fm"], k_fm[:, :], sDBG)
        b.wait("gpsimd", sDBG)
    if dbgout("v_tm", [128, TT * D]):
        b.wait("gpsimd", sV)
        b.dma("gpsimd", dbg["v_tm"], v_tm[:, :], sDBG)
        b.wait("gpsimd", sDBG)
    if dbgout("cc_in", [H * 2 * 128, HD]) and cc_dma_evts:
        b.waitev("sync", cc_dma_evts[-1])
        b.dma("sync", dbg["cc_in"], cc_in.ap()[:, :], sST)

    if last_phase == "p1":
        return finish()

    b.waitev("gpsimd", cc_dma_evts[-1])
    b.op("gpsimd", "collective_compute", "AllGather", ALU.bypass,
         [list(range(NCORES))], [cc_in.ap().opt()], [cc_out.ap().opt()],
         sem=sCC)
    ag_used = {0: Evt(), 1: Evt()}
    combine_done = {}

    def emit_combine():
        b.wait("sync", sCC, 1)
        for s in range(NCORES):
            for h in range(H):
                par = (s * H + h) % 2
                b.waitev("sync", ag_used[par])
                ld = b.dma(
                    "sync", agbuf2[par].rearrange("p (a c) -> p a c", c=HD),
                    cc_out.ap()[s, (h * 2) * 128:(h * 2 + 2) * 128, :]
                    .rearrange("(a p) c -> p a c", p=128), sAG[par])
                b.waitev("vector", ld)
                for pt in range(2):
                    srcv = agbuf2[par][:, pt * HD:(pt + 1) * HD]
                    cf = coef_sb[:, h * NCORES + s:h * NCORES + s + 1]
                    if s == 0:
                        ag_used[par] = b.op("vector", "tensor_scalar_mul",
                                            Sin_ap(h, pt), srcv, cf, sem=sV)
                    else:
                        ag_used[par] = b.op(
                            "vector", "scalar_tensor_tensor",
                            Sin_ap(h, pt), srcv, cf, Sin_ap(h, pt),
                            ALU.mult, ALU.add, sem=sV)
                if s == NCORES - 1:
                    combine_done[h] = ag_used[par]

    if dbgout("S_in", [128, 4 * 1024]):
        b.wait("gpsimd", sV)
        b.dma("gpsimd", dbg["S_in"], poolE[:, 12 * 1024:16 * 1024], sDBG)
        b.wait("gpsimd", sDBG)

    if last_phase == "p2":
        return finish()

    qdma = head_dma("wq", 0, Evt(sT, sT.n))
    mask_dma = {}
    mask_used = {}
    q_evac = [Evt(sS, sS.n)] * TT
    qrope_done = [Evt()] * TT
    qtr_done = [Evt(), Evt()]
    psD_free = [Evt(sV, sV.n)]
    psEF_free = [Evt(sV, sV.n)]
    b.wait("scalar", sV)
    b.wait("tensor", sV)
    for _s in range(4):
        b.wait("vector", sRSw[_s])
        b.wait("sync", sRSw[_s])
        b.wait("scalar", sRSw[_s])

    for h in range(H):
        b.waitev("sync", mask_used.get(h - 1))
        mask_dma[h] = b.dma(
            "sync", mask_sb.rearrange("p (a c) -> p a c", c=CS),
            mask_in[h].rearrange("(a p) c -> p a c", p=128), sMSK)
        bqev = bias_bcast(bb_q, 0, h, q_evac[TT - 1])
        b.waitev("tensor", qdma[0])
        b.waitev("tensor", qdma[1])
        for half in range(2):
            for tl in range(4):
                t = half * 4 + tl
                pst = (psA if t % 2 == 0 else psB)[:, 0:256]
                b.waitev("tensor",
                         q_evac[t - 2] if t >= 2 else q_evac[TT - 2 + t])
                proj_half(pst[:, 0:128], wbufA3, t)
                proj_half(pst[:, 128:256], wbufB3, t)
                b.wait("vector", sT)
                if t == 0:
                    b.waitev("vector", bqev)
                bv = b.op("vector", "tensor_add", pst, pst, bb_q, sem=sV)
                b.wait("scalar", sV, bv.n)
                q_evac[t] = b.op("scalar", "copy",
                                 ktmp[:, (t % 2) * 256:(t % 2) * 256 + 256],
                                 pst, sem=sS)
                b.waitev("vector", q_evac[t])
                if tl == 0:
                    b.waitev("vector", qtr_done[0])
                    b.waitev("vector", qtr_done[1])
                qrope_done[t] = rope6(
                    krot_s[:, tl * 256:(tl + 1) * 256],
                    ktmp[:, (t % 2) * 256:(t % 2) * 256 + 256], t)
            pst = psT1b if half == 0 else psT2b
            b.waitev("tensor", qrope_done[half * 4 + 3])
            b.waitev("tensor", psT_free[half])
            slot = 0
            for pt in range(2):
                for tl in range(4):
                    b.op("tensor", "transpose",
                         pst[:, slot * 128:(slot + 1) * 128],
                         krot_s[:, tl * 256 + pt * 128:
                                tl * 256 + pt * 128 + 128],
                         ident[:, :], sem=(sT if slot == 7 else None))
                    slot += 1
            qtr_done[half] = Evt(sT, sT.n)
            b.wait("scalar", sT)
            src = pst.rearrange("p (a b k) -> p a b k", a=2, k=128)
            dst = regH.rearrange("p (a k) -> p a k", k=TOK)[
                :, 0:2, (half * 4) * 128:(half * 4 + 4) * 128].rearrange(
                "p a (b k) -> p a b k", k=128)
            psT_free[half] = b.op("scalar", "copy", dst, src, sem=sS)
        q_ready = Evt(sS, sS.n)
        if h == 0:
            emit_combine()

        if h + 1 < H:
            qdma = head_dma("wq", h + 1, Evt(sT, sT.n))

        attn_last = Evt()
        for j in range(CH):
            b.waitev("vector", attn_last)
            f_ev = None
            if j == 0:
                b.waitev("vector", combine_done.get(h))
                for pt in range(2):
                    f_ev = b.op("vector", "tensor_copy",
                                F_sb[:, pt * HD:(pt + 1) * HD], Sin_ap(h, pt),
                                sem=sV)
            else:
                for pt in range(2):
                    f_ev = b.op("vector", "scalar_tensor_tensor",
                                F_sb[:, pt * HD:(pt + 1) * HD], Sin_ap(h, pt),
                                gpow[h][j], L_ap(h, j, pt), ALU.mult, ALU.add,
                                sem=sV)
            b.waitev("tensor", q_ready)
            b.waitev("tensor", psD_free[0])
            for kt in range(2):
                for pt in range(2):
                    b.op("tensor", "matmul", psD_[:, kt * 256:(kt + 1) * 256],
                         k_fm[:, (2 * h + pt) * TOK + (j * 2 + kt) * 128:
                              (2 * h + pt) * TOK + (j * 2 + kt) * 128 + 128],
                         regH[:, pt * TOK + j * 256:pt * TOK + (j + 1) * 256],
                         start=(pt == 0), stop=(pt == 1),
                         sem=(sT if (kt == 1 and pt == 1) else None))
            b.wait("vector", sT)
            b.waitev("vector", mask_dma[h])
            for kt in range(2):
                psD_free[0] = b.op("vector", "tensor_mul",
                                   simT_sb[:, kt * 256:(kt + 1) * 256],
                                   psD_[:, kt * 256:(kt + 1) * 256],
                                   mask_sb[:, kt * CS:(kt + 1) * CS], sem=sV)
            mask_used[h] = psD_free[0]
            b.waitev("tensor", psD_free[0])
            b.waitev("tensor", psEF_free[0])
            b.waitev("tensor", f_ev)
            for qm in range(2):
                for kt in range(2):
                    b.op("tensor", "matmul", psE[:, qm * 256:(qm + 1) * 256],
                         simT_sb[:, kt * 256 + qm * 128:
                                 kt * 256 + qm * 128 + 128],
                         v_tm[:, (2 * j + kt) * D + h * 256:
                              (2 * j + kt) * D + (h + 1) * 256],
                         start=(kt == 0), stop=(kt == 1))
                for dkh in range(2):
                    b.op("tensor", "matmul", psF[:, qm * 256:(qm + 1) * 256],
                         regH[:, dkh * TOK + j * 256 + qm * 128:
                              dkh * TOK + j * 256 + qm * 128 + 128],
                         F_sb[:, dkh * HD:(dkh + 1) * HD],
                         start=(dkh == 0), stop=(dkh == 1),
                         sem=(sT if qm == 1 and dkh == 1 else None))
            attn_last = Evt(sT, sT.n)
            b.wait("scalar", sT)
            ev = None
            for qm in range(2):
                ev = b.op("scalar", "mul", int_sc[:, qm * 256:(qm + 1) * 256],
                          psF[:, qm * 256:(qm + 1) * 256],
                          qdec_sb[:, h * 2 + qm:h * 2 + qm + 1],
                          sem=(sS if qm == 1 else None))
            b.wait("vector", sS, ev.n)
            for qm in range(2):
                psEF_free[0] = b.op("vector", "tensor_add",
                                    ret_sc[:, qm * 256:(qm + 1) * 256],
                                    int_sc[:, qm * 256:(qm + 1) * 256],
                                    psE[:, qm * 256:(qm + 1) * 256], sem=sV)
            b.wait("vector", sV)
            cb = (j % 2) * 16
            st_v = None
            for qm in range(2):
                st_v = b.op("vector", "reduce_sum",
                            ST[:, cb + qm:cb + qm + 1],
                            ret_sc[:, qm * 256:(qm + 1) * 256], AX, sem=sV)
            b.wait("scalar", sV, st_v.n)
            ev = None
            for qm in range(2):
                b.wait("scalar", sS)
                ev = b.op("scalar", "activation", p3sq[:, :],
                          ret_sc[:, qm * 256:(qm + 1) * 256], AF.Square,
                          accum_out=ST[:, cb + 2 + qm:cb + 3 + qm], sem=sS)
            b.wait("vector", sS, ev.n)
            b.op("vector", "tensor_scalar_mul", ST[:, cb + 4:cb + 6],
                 ST[:, cb:cb + 2], 1.0 / HD)
            me_ev = b.op("vector", "tensor_scalar", ST[:, cb + 6:cb + 8],
                         ST[:, cb + 2:cb + 4], 1.0 / HD, EPS,
                         ALU.mult, ALU.add, sem=sV)
            b.wait("scalar", sV, me_ev.n)
            m2_ev = b.op("scalar", "activation", ST[:, cb + 8:cb + 10],
                         ST[:, cb + 4:cb + 6], AF.Square, sem=sS)
            b.wait("vector", sS, m2_ev.n)
            var_ev = b.op("vector", "tensor_sub", ST[:, cb + 10:cb + 12],
                          ST[:, cb + 6:cb + 8], ST[:, cb + 8:cb + 10], sem=sV)
            b.wait("scalar", sV, var_ev.n)
            sq_ev = b.op("scalar", "activation", ST[:, cb + 12:cb + 14],
                         ST[:, cb + 10:cb + 12], AF.Sqrt, sem=sS)
            b.wait("vector", sS, sq_ev.n)
            b.op("vector", "reciprocal", ST[:, cb + 14:cb + 16],
                 ST[:, cb + 12:cb + 14], sem=sV)
            hazard_gap()
            for qm in range(2):
                b.op("vector", "tensor_scalar",
                     rhat_tm[:, (2 * j + qm) * D + h * 256:
                             (2 * j + qm) * D + (h + 1) * 256],
                     ret_sc[:, qm * 256:(qm + 1) * 256],
                     ST[:, cb + 4 + qm:cb + 5 + qm],
                     ST[:, cb + 14 + qm:cb + 15 + qm],
                     ALU.subtract, ALU.mult, sem=sV)

    if dbgout("rhat_tm", [128, TT * D]):
        b.wait("gpsimd", sV)
        b.dma("gpsimd", dbg["rhat_tm"], rhat_tm[:, :], sDBG)
        b.wait("gpsimd", sDBG)

    if last_phase == "p3":
        return finish()
    barrier()
    psT_free = [Evt(), Evt()]

    wbufO3 = wbufO.rearrange("p (d c) -> p d c", c=128)

    def half_dma(wname, d, ev_gate):
        b.waitev("gpsimd", ev_gate)
        odd = wbufO3 if wname == "wo" else wbufB3
        buf = wbufA3 if d % 2 == 0 else odd
        return b.dma("gpsimd", buf, w3[wname][:, :, d * 128:(d + 1) * 128],
                     sWA if d % 2 == 0 else sWB)

    g_dma = {0: half_dma("wg", 0, Evt(sT, sT.n)),
             1: half_dma("wg", 1, Evt(sT, sT.n))}
    tg_ready = [Evt(), Evt()]
    tg_cons = Evt()
    gate_last = Evt()
    psAB_free = [Evt(), Evt()]

    for d in range(DT):
        wb3 = wbufA3 if d % 2 == 0 else wbufB3
        b.waitev("tensor", g_dma[d])
        bank = d % 2
        pstb = psT1b if bank == 0 else psT2b
        b.waitev("tensor", psT_free[bank])
        for t in range(TT):
            b.op("tensor", "transpose", pstb[:, t * 128:(t + 1) * 128],
                 rhat_tm[:, t * D + d * 128:t * D + d * 128 + 128],
                 ident[:, :], sem=(sT if t == 7 else None))
        rt_ev = Evt(sT, sT.n)
        for n in range(2):
            pst = psA if n == 0 else psB
            b.waitev("tensor", psAB_free[n])
            for dt in range(DT):
                b.op("tensor", "matmul", pst[:, :],
                     wb3[:, dt, :],
                     xn_fm[:, dt * TOK + n * 512:dt * TOK + (n + 1) * 512],
                     start=(dt == 0), stop=(dt == DT - 1),
                     sem=(sT if dt == DT - 1 else None))
            b.wait("scalar", sT)
            b.waitev("scalar", tg_cons)
            tg_ready[n] = b.op("scalar", "activation",
                               tg_sb[:, n * 512:(n + 1) * 512],
                               pst[:, :], AF.Silu,
                               bias=bc_sb[:, d:d + 1], sem=sS)
        if d + 2 < DT:
            g_dma[d + 2] = half_dma("wg", d + 2, Evt(sT, sT.n))
        b.waitev("vector", rt_ev)
        b.waitev("vector", tg_cons)
        psT_free[bank] = b.op("vector", "tensor_scalar", rn_bf[:, :],
                              pstb[:, :], gn_sb[:, d:d + 1],
                              gn_sb[:, DT + d:DT + d + 1],
                              ALU.mult, ALU.add, sem=sV)
        b.wait("gpsimd", sV, psT_free[bank].n)
        b.wait("gpsimd", sS, tg_ready[1].n)
        for n in range(2):
            gate_last = b.op("gpsimd", "tensor_mul",
                             gate_ap(d)[:, n * 512:(n + 1) * 512],
                             rn_bf[:, n * 512:(n + 1) * 512],
                             tg_sb[:, n * 512:(n + 1) * 512], sem=sG)
        tg_cons = gate_last
        psAB_free[0] = psAB_free[1] = gate_last

    if dbgout("gate_fm", [128, DT * TOK]):
        b.wait("gpsimd", sG)
        b.wait("gpsimd", sV)
        b.dma("gpsimd", dbg["gate_fm"], poolE[:, 0:DT * TOK], sDBG)
        b.wait("gpsimd", sDBG)

    o_dma = {0: half_dma("wo", 0, Evt(sT, sT.n)),
             1: half_dma("wo", 1, Evt(sT, sT.n))}
    psO_free = [Evt(), Evt()]
    for d in range(DT):
        wb3 = wbufA3 if d % 2 == 0 else wbufO3
        b.waitev("tensor", o_dma[d])
        if d == 0:
            b.waitev("tensor", gate_last)
        for n in range(2):
            pst = psC if n == 0 else psE
            b.waitev("tensor", psO_free[n])
            for gt in range(DT):
                b.op("tensor", "matmul", pst[:, :],
                     wb3[:, gt, :],
                     gate_ap(gt)[:, n * 512:(n + 1) * 512],
                     start=(gt == 0), stop=(gt == DT - 1),
                     sem=(sT if gt == DT - 1 else None))
            b.wait("scalar", sT)
            psO_free[n] = b.op(
                "scalar", "add",
                s_fm[:, d * TOK + n * 512:d * TOK + (n + 1) * 512],
                pst[:, :], bc_sb[:, DT + d:DT + d + 1], sem=sS)
        if d + 2 < DT:
            o_dma[d + 2] = half_dma("wo", d + 2, Evt(sT, sT.n))

    if dbgout("s_fm", [128, DT * TOK]):
        b.wait("gpsimd", sS)
        b.dma("gpsimd", dbg["s_fm"], s_fm[:, :], sDBG)
        b.wait("gpsimd", sDBG)

    if last_phase == "gating":
        return finish()
    barrier()

    def gbuf8(p):
        return r8[:, p * 8192:(p + 1) * 8192].rearrange(
            "p (d c) -> p d c", c=512)
    def ubuf8(p):
        return r8[:, 16384 + p * 8192:16384 + (p + 1) * 8192].rearrange(
            "p (d c) -> p d c", c=512)
    def dbuf8(p):
        return r8[:, p * 16384:(p + 1) * 16384].rearrange(
            "p (d c) -> p d c", c=D)

    fuse_evac = {}
    down_evac = [Evt(), Evt()]
    last_sg = [Evt()]
    b.wait("tensor", sS)
    b.wait("tensor", sV)
    b.wait("tensor", sG)
    psgu_free = {0: Evt(), 1: Evt()}

    for Q in range(4):
        gu_free = {0: Evt(sT, sT.n), 1: Evt(sT, sT.n)}
        psgu_free = {0: down_evac[1], 1: down_evac[1]}
        for ic in range(12):
            par = ic % 2
            b.waitev("sync", gu_free[par])
            g1 = b.dma("sync", gbuf8(par),
                       g8d3[:, :, ic * 512:(ic + 1) * 512], sG8s[par])
            g2 = b.dma("sync", ubuf8(par),
                       u8d3[:, :, ic * 512:(ic + 1) * 512], sU8s[par])
            ready = (g1, g2)
            b.waitev("tensor", ready[0])
            b.waitev("tensor", ready[1])
            for i in range(4):
                it = ic * 4 + i
                ppar = it % 2
                psg = psA if ppar == 0 else psE
                psu = psB if ppar == 0 else psF
                b.waitev("tensor", psgu_free[ppar])
                for dt in range(DT):
                    b.op("tensor", "matmul", psg[:, 0:256],
                         gbuf8(par)[:, dt, i * 128:i * 128 + 128],
                         xn_fm[:, dt * TOK + Q * 256:dt * TOK + (Q + 1) * 256],
                         start=(dt == 0), stop=(dt == DT - 1))
                for dt in range(DT):
                    b.op("tensor", "matmul", psu[:, 0:256],
                         ubuf8(par)[:, dt, i * 128:i * 128 + 128],
                         xn_fm[:, dt * TOK + Q * 256:dt * TOK + (Q + 1) * 256],
                         start=(dt == 0), stop=(dt == DT - 1),
                         sem=(sT if dt == DT - 1 else None))
                sl = (it % 2) * 256
                b.wait("scalar", sT)
                b.waitev("scalar", fuse_evac.get(it - 2 if it >= 2
                                                 else 46 + it))
                b.op("scalar", "activation", tg_sb[:, sl:sl + 256],
                     psg[:, 0:256], AF.Silu, scale=sc_sb[:, it:it + 1])
                ev = b.op("scalar", "mul", tu_sb[:, sl:sl + 256],
                          psu[:, 0:256],
                          sc_sb[:, 48 + it:48 + it + 1], sem=sS)
                b.wait("vector", sS, ev.n)
                ev = b.op("vector", "tensor_mul", fuse_ap(it),
                          tg_sb[:, sl:sl + 256], tu_sb[:, sl:sl + 256],
                          sem=sV)
                fuse_evac[it] = ev
                psgu_free[ppar] = ev
            gu_free[par] = Evt(sT, sT.n)
        fuse_done = fuse_evac[47]
        b.waitev("tensor", fuse_done)
        for og in range(2):
            d_free = {0: Evt(sT, sT.n), 1: Evt(sT, sT.n)}
            b.waitev("tensor", down_evac[og])
            for dc in range(6):
                par = dc % 2
                b.waitev("sync", d_free[par])
                dn = b.dma("sync", dbuf8(par),
                           d8d3[:, dc * 8:(dc + 1) * 8, :], sD8s[par])
                b.waitev("tensor", dn)
                for it8 in range(8):
                    it = dc * 8 + it8
                    for m8 in range(8):
                        m = og * 8 + m8
                        pst = allps[m8][:, og * 256:og * 256 + 256]
                        b.op("tensor", "matmul", pst,
                             dbuf8(par)[:, it8, m * 128:m * 128 + 128],
                             fuse_ap(it),
                             start=(it == 0), stop=(it == 47),
                             sem=(sT if (it8 == 7 and m8 == 7) else None))
                d_free[par] = Evt(sT, sT.n)
            b.wait("scalar", sT)
            sg_evs = {-2: last_sg[0], -1: last_sg[0]}
            for m8 in range(8):
                m = og * 8 + m8
                pst = allps[m8][:, og * 256:og * 256 + 256]
                dsl = 512 + (m8 % 2) * 256
                b.waitev("scalar", sg_evs.get(m8 - 2))
                ev = b.op("scalar", "activation", tg_sb[:, dsl:dsl + 256],
                          pst, AF.Silu, scale=sc_sb[:, 96 + m:96 + m + 1],
                          sem=sS)
                b.wait("vector", sS, ev.n)
                sg_evs[m8] = b.op(
                    "vector", "tensor_mul",
                    s_fm[:, m * TOK + Q * 256:m * TOK + (Q + 1) * 256],
                    s_fm[:, m * TOK + Q * 256:m * TOK + (Q + 1) * 256],
                    tg_sb[:, dsl:dsl + 256], sem=sV)
            down_evac[og] = sg_evs[7]
            last_sg[0] = sg_evs[7]

    if dbgout("sg_fm", [128, DT * TOK]):
        b.wait("gpsimd", sV)
        b.dma("gpsimd", dbg["sg_fm"], s_fm[:, :], sDBG)
        b.wait("gpsimd", sDBG)

    if last_phase == "p4":
        return finish()
    barrier()

    out_done = [Evt(), Evt()]
    store_done = [Evt(), Evt()]
    add_done = [None] * TT
    for t in range(TT):
        par = t % 2
        if t >= 2:
            b.waitev("sync", add_done[t - 2])
        xl = b.dma("sync", xrow_v(par), x_in[t * 128:(t + 1) * 128, :], sX[par])
        for half in range(2):
            pstb = psT1b if half == 0 else psT2b
            b.waitev("tensor", out_done[half])
            for dd in range(8):
                d = half * 8 + dd
                b.op("tensor", "transpose", pstb[:, dd * 128:(dd + 1) * 128],
                     s_fm[:, d * TOK + t * 128:d * TOK + (t + 1) * 128],
                     ident[:, :], sem=(sT if dd == 7 else None))
            b.wait("vector", sT)
            b.waitev("vector", xl)
            if t >= 2 and half == 0:
                b.waitev("vector", store_done[par])
            src = pstb.rearrange("p (d k) -> p d k", k=128)
            xs = xrow_v(par).rearrange("p (d k) -> p d k", k=128)[
                :, half * 8:half * 8 + 8, :]
            ds = out_sb[:, par * D:(par + 1) * D].rearrange(
                "p (d k) -> p d k", k=128)[:, half * 8:half * 8 + 8, :]
            out_done[half] = b.op("vector", "tensor_add", ds, src, xs,
                                  sem=sV)
        add_done[t] = out_done[1]
        b.waitev("sync", out_done[1])
        store_done[par] = b.dma("sync", y_out[t * 128:(t + 1) * 128, :],
                                out_sb[:, par * D:(par + 1) * D], sSTp[par])

    return finish()



def host_tables(core):
    t0 = core * TOK
    thetas = 1.0 / 10000.0 ** np.linspace(0.0, 1.0, HD // 2)
    pos = np.arange(t0, t0 + TOK, dtype=np.float64)
    ang = pos[:, None] * thetas[None, :]
    sin = np.sin(ang).astype(np.float32)
    cos = np.cos(ang).astype(np.float32)
    g = 1.0 - np.exp(np.linspace(math.log(1 / 32), math.log(1 / 512), H))
    kt = np.arange(CS, dtype=np.float64)
    dist = kt[None, :] - kt[:, None]
    maskT = np.zeros((H, CS, CS), dtype=np.float32)
    for h in range(H):
        m = np.where(dist >= 0, g[h] ** np.maximum(dist, 0.0) / SCALE, 0.0)
        maskT[h] = m.astype(np.float32)
    maskT = maskT.reshape(H, 2 * 128, CS)
    kd = np.zeros((128, 2 * H), dtype=np.float32)
    qd = np.zeros((128, 2 * H), dtype=np.float32)
    for h in range(H):
        for ktile in range(2):
            idx = np.arange(ktile * 128, ktile * 128 + 128, dtype=np.float64)
            kd[:, h * 2 + ktile] = ((g[h] ** (CS - 1.0 - idx)) / SCALE)
            qd[:, h * 2 + ktile] = (g[h] ** (idx + 1.0))
    coef = np.zeros((128, H * NCORES), dtype=np.float32)
    for h in range(H):
        for s in range(NCORES):
            if s < core:
                coef[:, h * NCORES + s] = np.float32(
                    g[h] ** (1024.0 * (core - 1 - s)))
    return dict(sin=sin, cos=cos, maskT=maskT, kdec=kd, qdec=qd, coef=coef)


def col_layout(v):
    return np.ascontiguousarray(
        np.asarray(v, np.float32).reshape(-1, 128).T).astype(np.float32)




def make_inputs(inputs, core):
    t = host_tables(core)
    f32 = lambda a: np.ascontiguousarray(np.asarray(a, dtype=np.float32))
    gncols = np.concatenate([col_layout(inputs["L0_attn_gn_weight"]),
                             col_layout(inputs["L0_attn_gn_bias"])], 1)
    sccols = np.concatenate([col_layout(inputs["L0_ffn_gate_scale"]),
                             col_layout(inputs["L0_ffn_up_scale"]),
                             col_layout(inputs["L0_ffn_down_scale"])], 1)
    x = f32(inputs["L0_attn_x"]).reshape(NCORES * TOK, D)
    return {
        "x": np.ascontiguousarray(x[core * TOK:(core + 1) * TOK]),
        "wq": f32(inputs["L0_attn_w_q"]), "wk": f32(inputs["L0_attn_w_k"]),
        "wv": f32(inputs["L0_attn_w_v"]), "wg": f32(inputs["L0_attn_w_g"]),
        "wo": f32(inputs["L0_attn_w_o"]),
        "gate_w": f32(inputs["L0_ffn_gate_w"]),
        "up_w": f32(inputs["L0_ffn_up_w"]),
        "down_w": f32(inputs["L0_ffn_down_w"]),
        "biasf": np.stack([f32(inputs["L0_attn_b_q"]),
                           f32(inputs["L0_attn_b_k"]),
                           f32(inputs["L0_attn_b_v"])]),
        "bcols": np.concatenate([col_layout(inputs["L0_attn_b_g"]),
                                 col_layout(inputs["L0_attn_b_o"])], 1),
        "gncols": gncols, "sccols": sccols,
        "sin": t["sin"], "cos": t["cos"], "maskT": t["maskT"],
        "kdec": t["kdec"], "qdec": t["qdec"], "coef": t["coef"],
        "ident_t": np.eye(128, dtype=np.float32),
    }


_CACHE = {}


def kernel(**inputs):
    from concourse.bass_utils import run_bass_kernel_spmd
    if "prog" not in _CACHE:
        _CACHE["prog"] = build_program()
    nc, stack, _ = _CACHE["prog"]
    in_maps = [make_inputs(inputs, c) for c in range(NCORES)]
    res = run_bass_kernel_spmd(nc, in_maps, core_ids=list(range(NCORES)))
    y = np.concatenate([np.asarray(res.results[c]["y"], dtype=np.float32)
                        for c in range(NCORES)], axis=0)
    return y.reshape(1, NCORES * TOK, D)


# version 03 (final)
# speedup vs baseline: 1.1169x; 1.1169x over previous
import math
from contextlib import ExitStack

import numpy as np
import ml_dtypes
import concourse.bass as bass
import concourse.mybir as mybir

F32 = mybir.dt.float32
BF16 = mybir.dt.bfloat16
FP8 = mybir.dt.float8e4
AF = mybir.ActivationFunctionType
ALU = mybir.AluOpType
AX = mybir.AxisListType.X

NCORES = 8
TOK = 1024
TT = 8
D = 2048
DT = 16
H = 8
HD = 256
CH = 4
CS = 256
DI = 6144
SCALE = 16.0
EPS = 1e-5


class Sem:
    def __init__(self, h):
        self.h = h
        self.n = 0


class Evt:
    __slots__ = ("sem", "n")
    def __init__(self, sem=None, n=0):
        self.sem = sem
        self.n = n


class B:
    ENGINES = ("sync", "scalar", "vector", "tensor", "gpsimd")

    def __init__(self, nc, stack):
        self.nc = nc
        self.stack = stack
        self.ops = {e: [] for e in self.ENGINES}

    def sem(self, name):
        return Sem(self.stack.enter_context(self.nc.semaphore(name)))

    def emit(self, eng, fn):
        self.ops[eng].append(fn)

    def wait(self, eng, sem, val=None):
        v = sem.n if val is None else val
        assert v >= 0
        self.emit(eng, lambda e, s=sem.h, v=v: e.wait_ge(s, v))

    def waitev(self, eng, evt):
        if evt is not None and evt.sem is not None and evt.n > 0:
            self.wait(eng, evt.sem, evt.n)

    def dma(self, eng, out, in_, sem, **kw):
        sem.n += 16
        self.emit(eng, lambda e, o=out, i=in_, s=sem.h, kw=kw:
                  e.dma_start(out=o, in_=i, **kw).then_inc(s, 16))
        return Evt(sem, sem.n)

    def op(self, eng, meth, *args, sem=None, inc=1, **kw):
        if sem is not None:
            sem.n += inc
        def f(e, m=meth, a=args, kw=kw, s=sem, i=inc):
            inst = getattr(e, m)(*a, **kw)
            if s is not None:
                inst.then_inc(s.h, i)
        self.emit(eng, f)
        return Evt(sem, sem.n) if sem is not None else None


def build_program(debug=(), last_phase="p5", p1_heads=H, p1_mode="full"):
    stack = ExitStack()
    nc = bass.Bass("TRN2", target_bir_lowering=False, debug=False)
    b = B(nc, stack)

    def inp(name, shape, dt=F32):
        return nc.dram_tensor(name, list(shape), dt, kind="ExternalInput").ap()

    x_in = inp("x", [TOK, D])
    wq = inp("wq", [D, D]); wk = inp("wk", [D, D]); wv = inp("wv", [D, D])
    wg = inp("wg", [D, D]); wo = inp("wo", [D, D])
    gate_w = inp("gate_w", [D, DI]); up_w = inp("up_w", [D, DI])
    down_w = inp("down_w", [DI, D])
    biasf = inp("biasf", [3, D])
    bcols = inp("bcols", [128, 2 * DT])
    gncols_in = inp("gncols", [128, 2 * DT])
    sccols_in = inp("sccols", [128, 96 + DT])
    sin_in = inp("sin", [TOK, 128]); cos_in = inp("cos", [TOK, 128])
    mask_in = inp("maskT", [H, 2 * 128, CS])
    kdec_in = inp("kdec", [128, 2 * H])
    qdec_in = inp("qdec", [128, 2 * H])
    coef_in = inp("coef", [128, H * NCORES])
    ident_in = inp("ident_t", [128, 128])

    y_out = nc.dram_tensor("y", [TOK, D], F32, kind="ExternalOutput").ap()

    cc_in = nc.dram_tensor("cc_in", [H * 2 * 128, HD], F32)
    cc_out = nc.dram_tensor("cc_out", [NCORES, H * 2 * 128, HD], F32,
                            addr_space="Shared")
    g8_dram = nc.dram_tensor("g8_dram", [D, DI], FP8)
    u8_dram = nc.dram_tensor("u8_dram", [D, DI], FP8)
    d8_dram = nc.dram_tensor("d8_dram", [DI, D], FP8)
    g8d3 = g8_dram.ap().rearrange("(d p) c -> p d c", p=128)
    u8d3 = u8_dram.ap().rearrange("(d p) c -> p d c", p=128)
    d8d3 = d8_dram.ap().rearrange("(d p) c -> p d c", p=128)

    dbg = {}
    def dbgout(name, shape):
        if name in debug:
            dbg[name] = nc.dram_tensor("dbg_" + name, list(shape), F32,
                                       kind="ExternalOutput").ap()
        return dbg.get(name)

    sbt = lambda name, shape, dt: stack.enter_context(
        nc.sbuf_tensor(name, list(shape), dt)).ap()
    psf = lambda name: stack.enter_context(
        nc.psum_tensor(name, [128, 512], F32)).ap()

    xn_fm = sbt("xn_fm", [128, DT * TOK], BF16)
    k_fm = sbt("k_fm", [128, DT * TOK], BF16)
    s_fm = sbt("s_fm", [128, DT * TOK], BF16)
    v_tm = sbt("v_tm", [128, TT * D], BF16)
    rhat_tm = sbt("rhat_tm", [128, TT * D], BF16)
    poolE = sbt("poolE", [128, 16 * 1024], BF16)
    regH = sbt("regH", [128, 4 * 1024], BF16)
    wbufA = sbt("wbufA", [128, DT * 128], BF16)
    gn_sb = sbt("gn_sb", [128, 2 * DT], F32)
    bc_sb = sbt("bc_sb", [128, 2 * DT], F32)
    sc_sb = sbt("sc_sb", [128, 96 + DT], F32)
    kdec_sb = sbt("kdec_sb", [128, 2 * H], F32)
    qdec_sb = sbt("qdec_sb", [128, 2 * H], F32)
    coef_sb = sbt("coef_sb", [128, H * NCORES], F32)
    ident = sbt("ident", [128, 128], BF16)
    ST = sbt("ST", [128, 64], F32)

    sfmf = s_fm.bitcast(F32)
    ktmp = s_fm[:, 0:512]
    rtmp = s_fm[:, 512:768]
    krot_s = s_fm[:, 768:1792]
    simT_sb = s_fm[:, 1792:2304]
    F_sb = s_fm[:, 2304:2816]
    Ft_sb = s_fm[:, 2816:3328]
    ret_sc = sfmf[:, 1664:2176]
    int_sc = sfmf[:, 2176:2688]
    p3sq = sfmf[:, 2688:2944]
    Lcur = sfmf[:, 2944:3456]
    kd_sb = s_fm[:, 6912:7424]
    sin_sb = sfmf[:, 3712:4736]
    cos_sb = sfmf[:, 4736:5760]
    wbufB = s_fm[:, 11520:13568]
    mask_sb = sfmf[:, 6784:7296]
    agbuf = sfmf[:, 0:1024]
    agt_sb = s_fm[:, 2048:2560]
    bb_q = s_fm[:, 14592:14848]
    bb_k = s_fm[:, 14848:15104]
    bb_v = s_fm[:, 15104:15360]
    rn_bf = k_fm[:, 0:1024]
    tg_sb = k_fm[:, 1024:2048]
    tu_sb = k_fm[:, 2048:2560]
    wbufO = k_fm[:, 2560:4608]
    rhf = rhat_tm.bitcast(F32)
    xrow = rhf[:, 0:4096]
    sq_sc = rhf[:, 4096:6144]
    r8 = rhat_tm.bitcast(FP8)
    xn_scr = v_tm[:, 0:8192]
    out_sb = v_tm.bitcast(F32)[:, 0:4096]

    def L_ap(h, j, pt):
        off = ((h * 3 + (j - 1)) * 2 + pt) * HD
        return poolE[:, off:off + HD]
    def Sin_ap(h, pt):
        off = 12 * 1024 + (h * 2 + pt) * HD
        return poolE[:, off:off + HD]
    def gate_ap(d):
        return poolE[:, d * TOK:(d + 1) * TOK]
    def fuse_ap(i):
        return poolE[:, i * 256:(i + 1) * 256]

    psA, psB, psC, psD_, psE, psF, psT1, psT2 = [psf(f"ps{i}") for i in range(8)]
    psT1b = psT1.bitcast(BF16)
    psT2b = psT2.bitcast(BF16)
    allps = [psA, psB, psC, psD_, psE, psF, psT1, psT2]

    sW = b.sem("sW"); sHW = b.sem("sHW")
    sT = b.sem("sT"); sV = b.sem("sV"); sS = b.sem("sS")
    sCC = b.sem("sCC"); sST = b.sem("sST"); sBAR = b.sem("sBAR")
    sG = b.sem("sG")
    sPRE = b.sem("sPRE")
    sX = [b.sem("sX0"), b.sem("sX1")]
    sWA = b.sem("sWA"); sWB = b.sem("sWB")
    sBQ = b.sem("sBQ"); sBK = b.sem("sBK"); sBV = b.sem("sBV")
    sMSK = b.sem("sMSK"); sCCI = b.sem("sCCI")
    sSTp = [b.sem("sSTa"), b.sem("sSTb")]
    sAG = [b.sem("sAG0"), b.sem("sAG1")]
    sG8 = [b.sem("sG8a"), b.sem("sG8b")]
    sU8 = [b.sem("sU8a"), b.sem("sU8b")]
    sD8 = [b.sem("sD8a"), b.sem("sD8b")]
    sG8s = [b.sem("sG8sa"), b.sem("sG8sb")]
    sU8s = [b.sem("sU8sa"), b.sem("sU8sb")]
    sD8s = [b.sem("sD8sa"), b.sem("sD8sb")]
    sRW = [b.sem("sRW0"), b.sem("sRW1")]
    sIDE = b.sem("sIDE")
    sDBG = b.sem("sDBG")

    def finish():
        b.wait("sync", sST)
        b.wait("sync", sSTp[0])
        b.wait("sync", sSTp[1])
        b.wait("sync", sDBG)
        with nc.Block() as block:
            for eng in B.ENGINES:
                def run(e, eng=eng):
                    for f in b.ops[eng]:
                        f(e)
                getattr(block, eng)(run)
        return nc, stack, sorted(dbg.keys())

    def barrier():
        for e in B.ENGINES:
            sBAR.n += 1
            b.emit(e, lambda eng, s=sBAR.h: eng.sem_inc(s, 1))
        for e in B.ENGINES:
            b.wait(e, sBAR)

    def semtick(eng, sem):
        sem.n += 1
        b.emit(eng, lambda e, s=sem.h: e.sem_inc(s, 1))
        return Evt(sem, sem.n)

    def hazard_gap():
        ev = semtick("vector", sV)
        b.wait("scalar", sV, ev.n)
        es = semtick("scalar", sS)
        b.wait("vector", sS, es.n)

    w3 = {n: t.rearrange("(d p) c -> p d c", p=128) for n, t in
          [("wq", wq), ("wk", wk), ("wv", wv), ("wg", wg), ("wo", wo),
           ("gate_w", gate_w), ("up_w", up_w), ("down_w", down_w)]}
    wbufA3 = wbufA.rearrange("p (d c) -> p d c", c=128)
    wbufB3 = wbufB.rearrange("p (d c) -> p d c", c=128)


    b.dma("sync", sin_sb.rearrange("p (t c) -> p t c", c=128),
          sin_in.rearrange("(t p) c -> p t c", p=128), sPRE)
    b.dma("sync", cos_sb.rearrange("p (t c) -> p t c", c=128),
          cos_in.rearrange("(t p) c -> p t c", p=128), sPRE)
    b.dma("sync", gn_sb[:, :], gncols_in, sPRE)
    b.dma("sync", sc_sb[:, :], sccols_in, sPRE)
    b.dma("sync", kdec_sb[:, :], kdec_in, sPRE)
    b.dma("sync", qdec_sb[:, :], qdec_in, sPRE)
    b.dma("sync", coef_sb[:, :], coef_in, sPRE)
    b.dma("sync", bc_sb[:, :], bcols, sPRE)
    ident_ev = b.dma("gpsimd", ident[:, :], ident_in, sIDE)
    tables_ev = Evt(sPRE, sPRE.n)

    g_ = 1.0 - np.exp(np.linspace(math.log(1 / 32), math.log(1 / 512), H))
    g256 = [float(g_[h] ** 256) for h in range(H)]
    gpow = [[float(g_[h] ** (256.0 * j)) for j in range(CH)] for h in range(H)]

    psT_free = [Evt(), Evt()]

    def proj_half(pst, wbuf3, t):
        for dt in range(DT):
            b.op("tensor", "matmul", pst,
                 xn_fm[:, dt * TOK + t * 128:dt * TOK + (t + 1) * 128],
                 wbuf3[:, dt, :], start=(dt == 0), stop=(dt == DT - 1),
                 sem=(sT if dt == DT - 1 else None))
        return Evt(sT, sT.n)

    rope_gfree = [Evt()]

    def rope6(dst, src, t):
        cs = cos_sb[:, t * 128:(t + 1) * 128]
        sn = sin_sb[:, t * 128:(t + 1) * 128]
        de, do = dst[:, 0:256:2], dst[:, 1:256:2]
        se, so = src[:, 0:256:2], src[:, 1:256:2]
        tm1 = rtmp[:, 0:128]
        tm2 = rtmp[:, 128:256]
        b.waitev("vector", rope_gfree[0])
        b.op("vector", "tensor_mul", de, se, cs)
        b.op("vector", "tensor_mul", do, so, cs)
        b.op("vector", "tensor_mul", tm1, so, sn)
        ev = b.op("vector", "tensor_mul", tm2, se, sn, sem=sV)
        b.wait("gpsimd", sV, ev.n)
        b.op("gpsimd", "tensor_sub", de, de, tm1)
        rope_gfree[0] = b.op("gpsimd", "tensor_add", do, do, tm2, sem=sG)
        return rope_gfree[0]

    xrow_v = lambda par: xrow[:, par * D:(par + 1) * D]
    xnscr_v = lambda par: xn_scr[:, par * D:(par + 1) * D]
    norm_done = [None] * TT
    xload = [None] * TT
    stat_done = [None] * TT

    b.waitev("tensor", ident_ev)
    b.waitev("vector", tables_ev)

    red_done = [None] * TT
    for t in range(TT):
        if t >= 2:
            b.waitev("sync", stat_done[t - 2])
            b.waitev("sync", red_done[t - 2])
        xload[t] = b.dma("sync", xrow_v(t % 2),
                         x_in[t * 128:(t + 1) * 128, :], sX[t % 2])
        par = t % 2
        b.waitev("vector", xload[t])
        red_done[t] = b.op("vector", "reduce_sum", ST[:, t:t + 1],
                           xrow_v(par), AX, sem=sV)
        b.waitev("scalar", xload[t])
        b.wait("scalar", sS)
        stat_done[t] = b.op("scalar", "activation", sq_sc[:, :], xrow_v(par),
                            AF.Square, accum_out=ST[:, 8 + t:9 + t], sem=sS)
    b.wait("vector", sS)
    b.wait("vector", sV)
    b.op("vector", "tensor_scalar_mul", ST[:, 16:24], ST[:, 0:8], 1.0 / D)
    ev = b.op("vector", "tensor_scalar", ST[:, 24:32], ST[:, 8:16],
              1.0 / D, EPS, ALU.mult, ALU.add, sem=sV)
    b.wait("scalar", sV, ev.n)
    ev = b.op("scalar", "activation", ST[:, 32:40], ST[:, 16:24],
              AF.Square, sem=sS)
    b.wait("vector", sS, ev.n)
    ev = b.op("vector", "tensor_sub", ST[:, 40:48], ST[:, 24:32],
              ST[:, 32:40], sem=sV)
    b.wait("scalar", sV, ev.n)
    ev = b.op("scalar", "activation", ST[:, 48:56], ST[:, 40:48],
              AF.Sqrt, sem=sS)
    b.wait("vector", sS, ev.n)
    b.op("vector", "reciprocal", ST[:, 56:64], ST[:, 48:56], sem=sV)
    hazard_gap()
    hazard_gap()
    psT_free = [Evt(), Evt()]

    xscr_free = [Evt(), Evt()]
    for t in range(TT):
        par = t % 2
        if t >= 2:
            b.waitev("sync", norm_done[t - 2])
        else:
            b.waitev("sync", stat_done[TT - 2 + t])
            b.waitev("sync", red_done[TT - 2 + t])
        xload[t] = b.dma("sync", xrow_v(par),
                         x_in[t * 128:(t + 1) * 128, :], sX[par])
        b.waitev("vector", xload[t])
        if t >= 2:
            b.waitev("vector", xscr_free[par])
        norm_done[t] = b.op("vector", "tensor_scalar", xnscr_v(par),
                            xrow_v(par), ST[:, 16 + t:17 + t],
                            ST[:, 56 + t:57 + t],
                            ALU.subtract, ALU.mult, sem=sV)
        b.waitev("tensor", norm_done[t])
        for half in range(2):
            pst = psT1b if half == 0 else psT2b
            b.waitev("tensor", psT_free[half])
            for dd in range(8):
                d = half * 8 + dd
                b.op("tensor", "transpose", pst[:, dd * 128:(dd + 1) * 128],
                     xnscr_v(par)[:, d * 128:(d + 1) * 128], ident[:, :],
                     sem=(sT if dd == 7 else None))
            b.wait("scalar", sT)
            srcp = pst.rearrange("p (d k) -> p d k", k=128)
            dstp = xn_fm.rearrange("p (d k) -> p d k", k=TOK)[
                :, half * 8:half * 8 + 8, t * 128:(t + 1) * 128]
            psT_free[half] = b.op("scalar", "copy", dstp, srcp, sem=sS)
        xscr_free[par] = Evt(sT, sT.n)

    if dbgout("xn_fm", [128, DT * TOK]):
        b.wait("gpsimd", sS)
        b.dma("gpsimd", dbg["xn_fm"], xn_fm[:, :], sDBG)
        b.wait("gpsimd", sDBG)
    if dbgout("ST", [128, 64]):
        b.wait("sync", sV)
        b.dma("sync", dbg["ST"], ST[:, :], sDBG)
        b.wait("sync", sDBG)
    if dbgout("xn_tm", [128, 2 * D]):
        b.wait("gpsimd", sV)
        b.dma("gpsimd", dbg["xn_tm"], xn_scr[:, 0:2 * D], sDBG)
        b.wait("gpsimd", sDBG)

    if last_phase == "p0":
        return finish()
    if p1_mode != "nobar":
        barrier()
    psT_free = [Evt(), Evt()]

    def head_dma(wname, h, ev_gate):
        b.waitev("gpsimd", ev_gate)
        ea = b.dma("gpsimd", wbufA3, w3[wname][:, :, h * 256:h * 256 + 128], sWA)
        eb = b.dma("gpsimd", wbufB3,
                   w3[wname][:, :, h * 256 + 128:h * 256 + 256], sWB)
        return (ea, eb)

    k_evac = [Evt()] * TT
    v_evac = [Evt()] * TT
    rope_done = [Evt()] * TT
    transp_done = [Evt()] * H
    psC_free = [Evt()]
    dS_done = [Evt()]
    L_copy = [Evt()]
    L_dma = [Evt()]
    cc_dma_evts = []

    bias_sems = {0: sBQ, 1: sBK, 2: sBV}

    def bias_bcast(dst, bi, h, prev_use):
        b.waitev("gpsimd", prev_use)
        return b.dma("gpsimd", dst,
                     biasf[bi:bi + 1, h * 256:(h + 1) * 256].broadcast_to(
                         (128, 256)), bias_sems[bi])

    kdma = (Evt(), Evt())
    if p1_mode != "none":
        kdma = head_dma("wk", 0, ident_ev)
    for h in range(0 if p1_mode == "none" else p1_heads):
        bkev = bias_bcast(bb_k, 1, h, k_evac[TT - 1])
        bvev = bias_bcast(bb_v, 2, h, v_evac[TT - 1])
        if p1_mode == "dma":
            continue
        b.waitev("tensor", kdma[0])
        b.waitev("tensor", kdma[1])
        for t in range(TT):
            pst = (psA if t % 2 == 0 else psB)[:, 0:256]
            b.waitev("tensor", k_evac[t - 2] if t >= 2 else k_evac[TT - 2 + t])
            proj_half(pst[:, 0:128], wbufA3, t)
            proj_half(pst[:, 128:256], wbufB3, t)
            if p1_mode == "mm":
                k_evac[t] = Evt(sT, sT.n)
                rope_done[t] = k_evac[t]
                continue
            b.wait("vector", sT)
            if t == 0:
                b.waitev("vector", bkev)
            bv = b.op("vector", "tensor_add", pst, pst, bb_k, sem=sV)
            if p1_mode == "mmb":
                k_evac[t] = bv
                rope_done[t] = bv
                continue
            b.wait("scalar", sV, bv.n)
            k_evac[t] = b.op("scalar", "copy",
                             ktmp[:, (t % 2) * 256:(t % 2) * 256 + 256],
                             pst, sem=sS)
            if p1_mode == "noro":
                rope_done[t] = k_evac[t]
                continue
            b.waitev("vector", k_evac[t])
            if t == 0 and h > 0:
                b.waitev("vector", transp_done[h - 1])
            rope_done[t] = rope6(regH[:, t * 256:(t + 1) * 256],
                                 ktmp[:, (t % 2) * 256:(t % 2) * 256 + 256], t)
        if p1_mode == "k":
            continue
        vdma = head_dma("wv", h, Evt(sT, sT.n))
        b.waitev("tensor", vdma[0])
        b.waitev("tensor", vdma[1])
        for t in range(TT):
            pst = (psD_ if t % 2 == 0 else psE)[:, 0:256]
            b.waitev("tensor", v_evac[t - 2] if t >= 2 else v_evac[TT - 2 + t])
            proj_half(pst[:, 0:128], wbufA3, t)
            proj_half(pst[:, 128:256], wbufB3, t)
            b.wait("vector", sT)
            if t == 0:
                b.waitev("vector", bvev)
            v_evac[t] = b.op("vector", "tensor_add",
                             v_tm[:, t * D + h * 256:t * D + (h + 1) * 256],
                             pst, bb_v, sem=sV)
        if h + 1 < H:
            kdma = head_dma("wk", h + 1, Evt(sT, sT.n))
        if p1_mode == "kv":
            continue
        for half in range(2):
            pst = psT1b if half == 0 else psT2b
            b.waitev("tensor", rope_done[half * 4 + 3])
            b.waitev("tensor", psT_free[half])
            slot = 0
            for pt in range(2):
                for tl in range(4):
                    t = half * 4 + tl
                    b.op("tensor", "transpose",
                         pst[:, slot * 128:(slot + 1) * 128],
                         regH[:, t * 256 + pt * 128:t * 256 + pt * 128 + 128],
                         ident[:, :], sem=(sT if slot == 7 else None))
                    slot += 1
            b.wait("scalar", sT)
            src = pst.rearrange("p (a b k) -> p a b k", a=2, k=128)
            dst = k_fm.rearrange("p (a k) -> p a k", k=TOK)[
                :, 2 * h:2 * h + 2,
                (half * 4) * 128:(half * 4 + 4) * 128].rearrange(
                "p a (b k) -> p a b k", k=128)
            psT_free[half] = b.op("scalar", "copy", dst, src, sem=sS)
        transp_done[h] = Evt(sT, sT.n)
        if p1_mode == "kvt":
            continue
        b.waitev("vector", L_dma[0])
        b.op("vector", "memset", Lcur[:, :], 0.0)
        for j in range(CH):
            b.waitev("vector", L_copy[0])
            b.waitev("vector", dS_done[0])
            kd_ev = None
            for kt in range(2):
                kd_ev = b.op("vector", "tensor_scalar_mul",
                             kd_sb[:, kt * 256:(kt + 1) * 256],
                             regH[:, (2 * j + kt) * 256:(2 * j + kt + 1) * 256],
                             kdec_sb[:, h * 2 + kt:h * 2 + kt + 1], sem=sV)
            b.waitev("tensor", kd_ev)
            b.waitev("tensor", psC_free[0])
            for dkh in range(2):
                for kt in range(2):
                    b.op("tensor", "matmul", psC[:, dkh * 256:(dkh + 1) * 256],
                         kd_sb[:, kt * 256 + dkh * 128:
                               kt * 256 + dkh * 128 + 128],
                         v_tm[:, (2 * j + kt) * D + h * 256:
                              (2 * j + kt) * D + (h + 1) * 256],
                         start=(kt == 0), stop=(kt == 1),
                         sem=(sT if (dkh == 1 and kt == 1) else None))
            dS_done[0] = Evt(sT, sT.n)
            b.wait("vector", sT)
            for dkh in range(2):
                psC_free[0] = b.op("vector", "scalar_tensor_tensor",
                                   Lcur[:, dkh * 256:(dkh + 1) * 256],
                                   Lcur[:, dkh * 256:(dkh + 1) * 256], g256[h],
                                   psC[:, dkh * 256:(dkh + 1) * 256],
                                   ALU.mult, ALU.add, sem=sV)
            if j < CH - 1:
                b.waitev("scalar", psC_free[0])
                for pt in range(2):
                    L_copy[0] = b.op("scalar", "copy", L_ap(h, j + 1, pt),
                                     Lcur[:, pt * 256:(pt + 1) * 256], sem=sS)
            else:
                b.waitev("sync", psC_free[0])
                for pt in range(2):
                    L_dma[0] = b.dma(
                        "sync", cc_in.ap()[(h * 2 + pt) * 128:
                                           (h * 2 + pt + 1) * 128, :],
                        Lcur[:, pt * 256:(pt + 1) * 256], sCCI)
                cc_dma_evts.append(L_dma[0])

    if dbgout("k_fm", [128, DT * TOK]):
        b.wait("gpsimd", sS)
        b.wait("gpsimd", sV)
        b.dma("gpsimd", dbg["k_fm"], k_fm[:, :], sDBG)
        b.wait("gpsimd", sDBG)
    if dbgout("v_tm", [128, TT * D]):
        b.wait("gpsimd", sV)
        b.dma("gpsimd", dbg["v_tm"], v_tm[:, :], sDBG)
        b.wait("gpsimd", sDBG)
    if dbgout("cc_in", [H * 2 * 128, HD]) and cc_dma_evts:
        b.waitev("sync", cc_dma_evts[-1])
        b.dma("sync", dbg["cc_in"], cc_in.ap()[:, :], sST)

    if last_phase == "p1":
        return finish()

    b.waitev("gpsimd", cc_dma_evts[-1])
    b.op("gpsimd", "collective_compute", "AllGather", ALU.bypass,
         [list(range(NCORES))], [cc_in.ap().opt()], [cc_out.ap().opt()],
         sem=sCC)
    b.wait("sync", sCC, 1)
    ag_used = {0: Evt(), 1: Evt()}
    for s in range(NCORES):
        for h in range(H):
            par = (s * H + h) % 2
            b.waitev("sync", ag_used[par])
            ld = b.dma(
                "sync", agbuf.rearrange("p (a c) -> p a c", c=HD)[
                    :, par * 2:par * 2 + 2, :],
                cc_out.ap()[s, (h * 2) * 128:(h * 2 + 2) * 128, :].rearrange(
                    "(a p) c -> p a c", p=128), sAG[par])
            b.waitev("vector", ld)
            for pt in range(2):
                srcv = agbuf[:, (par * 2 + pt) * HD:(par * 2 + pt + 1) * HD]
                cf = coef_sb[:, h * NCORES + s:h * NCORES + s + 1]
                if s == 0:
                    ag_used[par] = b.op("vector", "tensor_scalar_mul",
                                        Sin_ap(h, pt), srcv, cf, sem=sV)
                else:
                    ag_used[par] = b.op("vector", "scalar_tensor_tensor",
                                        Sin_ap(h, pt), srcv, cf, Sin_ap(h, pt),
                                        ALU.mult, ALU.add, sem=sV)

    if dbgout("S_in", [128, 4 * 1024]):
        b.wait("gpsimd", sV)
        b.dma("gpsimd", dbg["S_in"], poolE[:, 12 * 1024:16 * 1024], sDBG)
        b.wait("gpsimd", sDBG)

    if last_phase == "p2":
        return finish()
    barrier()
    psT_free = [Evt(), Evt()]

    qdma = head_dma("wq", 0, Evt(sT, sT.n))
    mask_dma = {}
    mask_used = {}
    q_evac = [Evt()] * TT
    qrope_done = [Evt()] * TT
    qtr_done = [Evt(), Evt()]
    psD_free = [Evt()]
    psEF_free = [Evt()]

    for h in range(H):
        b.waitev("sync", mask_used.get(h - 1))
        mask_dma[h] = b.dma(
            "sync", mask_sb.rearrange("p (a c) -> p a c", c=CS),
            mask_in[h].rearrange("(a p) c -> p a c", p=128), sMSK)
        bqev = bias_bcast(bb_q, 0, h, q_evac[TT - 1])
        b.waitev("tensor", qdma[0])
        b.waitev("tensor", qdma[1])
        for half in range(2):
            for tl in range(4):
                t = half * 4 + tl
                pst = (psA if t % 2 == 0 else psB)[:, 0:256]
                b.waitev("tensor",
                         q_evac[t - 2] if t >= 2 else q_evac[TT - 2 + t])
                proj_half(pst[:, 0:128], wbufA3, t)
                proj_half(pst[:, 128:256], wbufB3, t)
                b.wait("vector", sT)
                if t == 0:
                    b.waitev("vector", bqev)
                bv = b.op("vector", "tensor_add", pst, pst, bb_q, sem=sV)
                b.wait("scalar", sV, bv.n)
                q_evac[t] = b.op("scalar", "copy",
                                 ktmp[:, (t % 2) * 256:(t % 2) * 256 + 256],
                                 pst, sem=sS)
                b.waitev("vector", q_evac[t])
                if tl == 0:
                    b.waitev("vector", qtr_done[0])
                    b.waitev("vector", qtr_done[1])
                qrope_done[t] = rope6(
                    krot_s[:, tl * 256:(tl + 1) * 256],
                    ktmp[:, (t % 2) * 256:(t % 2) * 256 + 256], t)
            pst = psT1b if half == 0 else psT2b
            b.waitev("tensor", qrope_done[half * 4 + 3])
            b.waitev("tensor", psT_free[half])
            slot = 0
            for pt in range(2):
                for tl in range(4):
                    b.op("tensor", "transpose",
                         pst[:, slot * 128:(slot + 1) * 128],
                         krot_s[:, tl * 256 + pt * 128:
                                tl * 256 + pt * 128 + 128],
                         ident[:, :], sem=(sT if slot == 7 else None))
                    slot += 1
            qtr_done[half] = Evt(sT, sT.n)
            b.wait("scalar", sT)
            src = pst.rearrange("p (a b k) -> p a b k", a=2, k=128)
            dst = regH.rearrange("p (a k) -> p a k", k=TOK)[
                :, 0:2, (half * 4) * 128:(half * 4 + 4) * 128].rearrange(
                "p a (b k) -> p a b k", k=128)
            psT_free[half] = b.op("scalar", "copy", dst, src, sem=sS)
        q_ready = Evt(sS, sS.n)

        if h + 1 < H:
            qdma = head_dma("wq", h + 1, Evt(sT, sT.n))

        attn_last = Evt()
        for j in range(CH):
            b.waitev("vector", attn_last)
            f_ev = None
            if j == 0:
                for pt in range(2):
                    f_ev = b.op("vector", "tensor_copy",
                                F_sb[:, pt * HD:(pt + 1) * HD], Sin_ap(h, pt),
                                sem=sV)
            else:
                for pt in range(2):
                    f_ev = b.op("vector", "scalar_tensor_tensor",
                                F_sb[:, pt * HD:(pt + 1) * HD], Sin_ap(h, pt),
                                gpow[h][j], L_ap(h, j, pt), ALU.mult, ALU.add,
                                sem=sV)
            b.waitev("tensor", q_ready)
            b.waitev("tensor", psD_free[0])
            for kt in range(2):
                for pt in range(2):
                    b.op("tensor", "matmul", psD_[:, kt * 256:(kt + 1) * 256],
                         k_fm[:, (2 * h + pt) * TOK + (j * 2 + kt) * 128:
                              (2 * h + pt) * TOK + (j * 2 + kt) * 128 + 128],
                         regH[:, pt * TOK + j * 256:pt * TOK + (j + 1) * 256],
                         start=(pt == 0), stop=(pt == 1),
                         sem=(sT if (kt == 1 and pt == 1) else None))
            b.wait("vector", sT)
            b.waitev("vector", mask_dma[h])
            for kt in range(2):
                psD_free[0] = b.op("vector", "tensor_mul",
                                   simT_sb[:, kt * 256:(kt + 1) * 256],
                                   psD_[:, kt * 256:(kt + 1) * 256],
                                   mask_sb[:, kt * CS:(kt + 1) * CS], sem=sV)
            mask_used[h] = psD_free[0]
            b.waitev("tensor", psD_free[0])
            b.waitev("tensor", psEF_free[0])
            b.waitev("tensor", f_ev)
            for qm in range(2):
                for kt in range(2):
                    b.op("tensor", "matmul", psE[:, qm * 256:(qm + 1) * 256],
                         simT_sb[:, kt * 256 + qm * 128:
                                 kt * 256 + qm * 128 + 128],
                         v_tm[:, (2 * j + kt) * D + h * 256:
                              (2 * j + kt) * D + (h + 1) * 256],
                         start=(kt == 0), stop=(kt == 1))
                for dkh in range(2):
                    b.op("tensor", "matmul", psF[:, qm * 256:(qm + 1) * 256],
                         regH[:, dkh * TOK + j * 256 + qm * 128:
                              dkh * TOK + j * 256 + qm * 128 + 128],
                         F_sb[:, dkh * HD:(dkh + 1) * HD],
                         start=(dkh == 0), stop=(dkh == 1),
                         sem=(sT if qm == 1 and dkh == 1 else None))
            attn_last = Evt(sT, sT.n)
            b.wait("scalar", sT)
            ev = None
            for qm in range(2):
                ev = b.op("scalar", "mul", int_sc[:, qm * 256:(qm + 1) * 256],
                          psF[:, qm * 256:(qm + 1) * 256],
                          qdec_sb[:, h * 2 + qm:h * 2 + qm + 1],
                          sem=(sS if qm == 1 else None))
            b.wait("vector", sS, ev.n)
            for qm in range(2):
                psEF_free[0] = b.op("vector", "tensor_add",
                                    ret_sc[:, qm * 256:(qm + 1) * 256],
                                    int_sc[:, qm * 256:(qm + 1) * 256],
                                    psE[:, qm * 256:(qm + 1) * 256], sem=sV)
            b.wait("vector", sV)
            cb = (j % 2) * 16
            st_v = None
            for qm in range(2):
                st_v = b.op("vector", "reduce_sum",
                            ST[:, cb + qm:cb + qm + 1],
                            ret_sc[:, qm * 256:(qm + 1) * 256], AX, sem=sV)
            b.wait("scalar", sV, st_v.n)
            ev = None
            for qm in range(2):
                b.wait("scalar", sS)
                ev = b.op("scalar", "activation", p3sq[:, :],
                          ret_sc[:, qm * 256:(qm + 1) * 256], AF.Square,
                          accum_out=ST[:, cb + 2 + qm:cb + 3 + qm], sem=sS)
            b.wait("vector", sS, ev.n)
            b.op("vector", "tensor_scalar_mul", ST[:, cb + 4:cb + 6],
                 ST[:, cb:cb + 2], 1.0 / HD)
            me_ev = b.op("vector", "tensor_scalar", ST[:, cb + 6:cb + 8],
                         ST[:, cb + 2:cb + 4], 1.0 / HD, EPS,
                         ALU.mult, ALU.add, sem=sV)
            b.wait("scalar", sV, me_ev.n)
            m2_ev = b.op("scalar", "activation", ST[:, cb + 8:cb + 10],
                         ST[:, cb + 4:cb + 6], AF.Square, sem=sS)
            b.wait("vector", sS, m2_ev.n)
            var_ev = b.op("vector", "tensor_sub", ST[:, cb + 10:cb + 12],
                          ST[:, cb + 6:cb + 8], ST[:, cb + 8:cb + 10], sem=sV)
            b.wait("scalar", sV, var_ev.n)
            sq_ev = b.op("scalar", "activation", ST[:, cb + 12:cb + 14],
                         ST[:, cb + 10:cb + 12], AF.Sqrt, sem=sS)
            b.wait("vector", sS, sq_ev.n)
            b.op("vector", "reciprocal", ST[:, cb + 14:cb + 16],
                 ST[:, cb + 12:cb + 14], sem=sV)
            hazard_gap()
            for qm in range(2):
                b.op("vector", "tensor_scalar",
                     rhat_tm[:, (2 * j + qm) * D + h * 256:
                             (2 * j + qm) * D + (h + 1) * 256],
                     ret_sc[:, qm * 256:(qm + 1) * 256],
                     ST[:, cb + 4 + qm:cb + 5 + qm],
                     ST[:, cb + 14 + qm:cb + 15 + qm],
                     ALU.subtract, ALU.mult, sem=sV)

    if dbgout("rhat_tm", [128, TT * D]):
        b.wait("gpsimd", sV)
        b.dma("gpsimd", dbg["rhat_tm"], rhat_tm[:, :], sDBG)
        b.wait("gpsimd", sDBG)

    if last_phase == "p3":
        return finish()
    barrier()
    psT_free = [Evt(), Evt()]

    wbufO3 = wbufO.rearrange("p (d c) -> p d c", c=128)

    def half_dma(wname, d, ev_gate):
        b.waitev("gpsimd", ev_gate)
        odd = wbufO3 if wname == "wo" else wbufB3
        buf = wbufA3 if d % 2 == 0 else odd
        return b.dma("gpsimd", buf, w3[wname][:, :, d * 128:(d + 1) * 128],
                     sWA if d % 2 == 0 else sWB)

    g_dma = {0: half_dma("wg", 0, Evt(sT, sT.n)),
             1: half_dma("wg", 1, Evt(sT, sT.n))}
    tg_ready = [Evt(), Evt()]
    tg_cons = Evt()
    gate_last = Evt()
    psAB_free = [Evt(), Evt()]

    for d in range(DT):
        wb3 = wbufA3 if d % 2 == 0 else wbufB3
        b.waitev("tensor", g_dma[d])
        bank = d % 2
        pstb = psT1b if bank == 0 else psT2b
        b.waitev("tensor", psT_free[bank])
        for t in range(TT):
            b.op("tensor", "transpose", pstb[:, t * 128:(t + 1) * 128],
                 rhat_tm[:, t * D + d * 128:t * D + d * 128 + 128],
                 ident[:, :], sem=(sT if t == 7 else None))
        rt_ev = Evt(sT, sT.n)
        for n in range(2):
            pst = psA if n == 0 else psB
            b.waitev("tensor", psAB_free[n])
            for dt in range(DT):
                b.op("tensor", "matmul", pst[:, :],
                     wb3[:, dt, :],
                     xn_fm[:, dt * TOK + n * 512:dt * TOK + (n + 1) * 512],
                     start=(dt == 0), stop=(dt == DT - 1),
                     sem=(sT if dt == DT - 1 else None))
            b.wait("scalar", sT)
            b.waitev("scalar", tg_cons)
            tg_ready[n] = b.op("scalar", "activation",
                               tg_sb[:, n * 512:(n + 1) * 512],
                               pst[:, :], AF.Silu,
                               bias=bc_sb[:, d:d + 1], sem=sS)
        if d + 2 < DT:
            g_dma[d + 2] = half_dma("wg", d + 2, Evt(sT, sT.n))
        b.waitev("vector", rt_ev)
        b.waitev("vector", tg_cons)
        psT_free[bank] = b.op("vector", "tensor_scalar", rn_bf[:, :],
                              pstb[:, :], gn_sb[:, d:d + 1],
                              gn_sb[:, DT + d:DT + d + 1],
                              ALU.mult, ALU.add, sem=sV)
        b.wait("gpsimd", sV, psT_free[bank].n)
        b.wait("gpsimd", sS, tg_ready[1].n)
        for n in range(2):
            gate_last = b.op("gpsimd", "tensor_mul",
                             gate_ap(d)[:, n * 512:(n + 1) * 512],
                             rn_bf[:, n * 512:(n + 1) * 512],
                             tg_sb[:, n * 512:(n + 1) * 512], sem=sG)
        tg_cons = gate_last
        psAB_free[0] = psAB_free[1] = gate_last

    if dbgout("gate_fm", [128, DT * TOK]):
        b.wait("gpsimd", sG)
        b.wait("gpsimd", sV)
        b.dma("gpsimd", dbg["gate_fm"], poolE[:, 0:DT * TOK], sDBG)
        b.wait("gpsimd", sDBG)

    o_dma = {0: half_dma("wo", 0, Evt(sT, sT.n)),
             1: half_dma("wo", 1, Evt(sT, sT.n))}
    psO_free = [Evt(), Evt()]
    for d in range(DT):
        wb3 = wbufA3 if d % 2 == 0 else wbufO3
        b.waitev("tensor", o_dma[d])
        if d == 0:
            b.waitev("tensor", gate_last)
        for n in range(2):
            pst = psC if n == 0 else psE
            b.waitev("tensor", psO_free[n])
            for gt in range(DT):
                b.op("tensor", "matmul", pst[:, :],
                     wb3[:, gt, :],
                     gate_ap(gt)[:, n * 512:(n + 1) * 512],
                     start=(gt == 0), stop=(gt == DT - 1),
                     sem=(sT if gt == DT - 1 else None))
            b.wait("scalar", sT)
            psO_free[n] = b.op(
                "scalar", "add",
                s_fm[:, d * TOK + n * 512:d * TOK + (n + 1) * 512],
                pst[:, :], bc_sb[:, DT + d:DT + d + 1], sem=sS)
        if d + 2 < DT:
            o_dma[d + 2] = half_dma("wo", d + 2, Evt(sT, sT.n))

    if dbgout("s_fm", [128, DT * TOK]):
        b.wait("gpsimd", sS)
        b.dma("gpsimd", dbg["s_fm"], s_fm[:, :], sDBG)
        b.wait("gpsimd", sDBG)

    if last_phase == "gating":
        return finish()
    barrier()

    def gbuf8(p):
        return r8[:, p * 8192:(p + 1) * 8192].rearrange(
            "p (d c) -> p d c", c=512)
    def ubuf8(p):
        return r8[:, 16384 + p * 8192:16384 + (p + 1) * 8192].rearrange(
            "p (d c) -> p d c", c=512)
    def dbuf8(p):
        return r8[:, p * 16384:(p + 1) * 16384].rearrange(
            "p (d c) -> p d c", c=D)

    fuse_evac = {}
    down_evac = [Evt(), Evt()]
    d_rst = {0: Evt(), 1: Evt()}
    last_sg = [Evt()]
    gu_restaged = {0: Evt(), 1: Evt()}
    psgu_free = {0: Evt(), 1: Evt()}

    for Q in range(4):
        gu_free = {0: Evt(sT, sT.n), 1: Evt(sT, sT.n)}
        psgu_free = {0: down_evac[1], 1: down_evac[1]}
        for ic in range(12):
            par = ic % 2
            if Q == 0:
                b.waitev("gpsimd", gu_free[par])
                b.waitev("gpsimd", gu_restaged[par])
                gdma = b.dma("gpsimd", gbuf8(par),
                             w3["gate_w"][:, :, ic * 512:(ic + 1) * 512],
                             sG8[par])
                udma = b.dma("gpsimd", ubuf8(par),
                             w3["up_w"][:, :, ic * 512:(ic + 1) * 512],
                             sU8[par])
                b.waitev("sync", gdma)
                b.waitev("sync", udma)
                b.dma("sync", g8d3[:, :, ic * 512:(ic + 1) * 512],
                      gbuf8(par), sRW[par])
                gu_restaged[par] = b.dma(
                    "sync", u8d3[:, :, ic * 512:(ic + 1) * 512],
                    ubuf8(par), sRW[par])
                ready = (gdma, udma)
            else:
                b.waitev("sync", gu_free[par])
                g1 = b.dma("sync", gbuf8(par),
                           g8d3[:, :, ic * 512:(ic + 1) * 512], sG8s[par])
                g2 = b.dma("sync", ubuf8(par),
                           u8d3[:, :, ic * 512:(ic + 1) * 512], sU8s[par])
                ready = (g1, g2)
            b.waitev("tensor", ready[0])
            b.waitev("tensor", ready[1])
            for i in range(4):
                it = ic * 4 + i
                ppar = it % 2
                psg = psA if ppar == 0 else psE
                psu = psB if ppar == 0 else psF
                b.waitev("tensor", psgu_free[ppar])
                for dt in range(DT):
                    b.op("tensor", "matmul", psg[:, 0:256],
                         gbuf8(par)[:, dt, i * 128:i * 128 + 128],
                         xn_fm[:, dt * TOK + Q * 256:dt * TOK + (Q + 1) * 256],
                         start=(dt == 0), stop=(dt == DT - 1))
                for dt in range(DT):
                    b.op("tensor", "matmul", psu[:, 0:256],
                         ubuf8(par)[:, dt, i * 128:i * 128 + 128],
                         xn_fm[:, dt * TOK + Q * 256:dt * TOK + (Q + 1) * 256],
                         start=(dt == 0), stop=(dt == DT - 1),
                         sem=(sT if dt == DT - 1 else None))
                sl = (it % 2) * 256
                b.wait("scalar", sT)
                b.waitev("scalar", fuse_evac.get(it - 2 if it >= 2
                                                 else 46 + it))
                b.op("scalar", "activation", tg_sb[:, sl:sl + 256],
                     psg[:, 0:256], AF.Silu, scale=sc_sb[:, it:it + 1])
                ev = b.op("scalar", "mul", tu_sb[:, sl:sl + 256],
                          psu[:, 0:256],
                          sc_sb[:, 48 + it:48 + it + 1], sem=sS)
                b.wait("vector", sS, ev.n)
                ev = b.op("vector", "tensor_mul", fuse_ap(it),
                          tg_sb[:, sl:sl + 256], tu_sb[:, sl:sl + 256],
                          sem=sV)
                fuse_evac[it] = ev
                psgu_free[ppar] = ev
            gu_free[par] = Evt(sT, sT.n)
        fuse_done = fuse_evac[47]
        b.waitev("tensor", fuse_done)
        for og in range(2):
            d_free = {0: Evt(sT, sT.n), 1: Evt(sT, sT.n)}
            b.waitev("tensor", down_evac[og])
            for dc in range(6):
                par = dc % 2
                if Q == 0 and og == 0:
                    b.waitev("gpsimd", d_free[par])
                    b.waitev("gpsimd", gu_restaged[0])
                    b.waitev("gpsimd", gu_restaged[1])
                    b.waitev("gpsimd", d_rst[par])
                    dn = b.dma("gpsimd", dbuf8(par),
                               w3["down_w"][:, dc * 8:(dc + 1) * 8, :],
                               sD8[par])
                    b.waitev("sync", dn)
                    b.dma("sync", d8d3[:, dc * 8:(dc + 1) * 8, :], dbuf8(par),
                          sRW[par])
                    d_rst[par] = Evt(sRW[par], sRW[par].n)
                else:
                    b.waitev("sync", d_free[par])
                    if Q == 0 and og == 1:
                        b.waitev("sync", Evt(sRW[par], sRW[par].n))
                    dn = b.dma("sync", dbuf8(par),
                               d8d3[:, dc * 8:(dc + 1) * 8, :], sD8s[par])
                b.waitev("tensor", dn)
                for it8 in range(8):
                    it = dc * 8 + it8
                    for m8 in range(8):
                        m = og * 8 + m8
                        pst = allps[m8][:, og * 256:og * 256 + 256]
                        b.op("tensor", "matmul", pst,
                             dbuf8(par)[:, it8, m * 128:m * 128 + 128],
                             fuse_ap(it),
                             start=(it == 0), stop=(it == 47),
                             sem=(sT if (it8 == 7 and m8 == 7) else None))
                d_free[par] = Evt(sT, sT.n)
            b.wait("scalar", sT)
            sg_evs = {-2: last_sg[0], -1: last_sg[0]}
            for m8 in range(8):
                m = og * 8 + m8
                pst = allps[m8][:, og * 256:og * 256 + 256]
                dsl = 512 + (m8 % 2) * 256
                b.waitev("scalar", sg_evs.get(m8 - 2))
                ev = b.op("scalar", "activation", tg_sb[:, dsl:dsl + 256],
                          pst, AF.Silu, scale=sc_sb[:, 96 + m:96 + m + 1],
                          sem=sS)
                b.wait("vector", sS, ev.n)
                sg_evs[m8] = b.op(
                    "vector", "tensor_mul",
                    s_fm[:, m * TOK + Q * 256:m * TOK + (Q + 1) * 256],
                    s_fm[:, m * TOK + Q * 256:m * TOK + (Q + 1) * 256],
                    tg_sb[:, dsl:dsl + 256], sem=sV)
            down_evac[og] = sg_evs[7]
            last_sg[0] = sg_evs[7]

    if dbgout("sg_fm", [128, DT * TOK]):
        b.wait("gpsimd", sV)
        b.dma("gpsimd", dbg["sg_fm"], s_fm[:, :], sDBG)
        b.wait("gpsimd", sDBG)

    if last_phase == "p4":
        return finish()
    barrier()

    out_done = [Evt(), Evt()]
    store_done = [Evt(), Evt()]
    add_done = [None] * TT
    for t in range(TT):
        par = t % 2
        if t >= 2:
            b.waitev("sync", add_done[t - 2])
        xl = b.dma("sync", xrow_v(par), x_in[t * 128:(t + 1) * 128, :], sX[par])
        for half in range(2):
            pstb = psT1b if half == 0 else psT2b
            b.waitev("tensor", out_done[half])
            for dd in range(8):
                d = half * 8 + dd
                b.op("tensor", "transpose", pstb[:, dd * 128:(dd + 1) * 128],
                     s_fm[:, d * TOK + t * 128:d * TOK + (t + 1) * 128],
                     ident[:, :], sem=(sT if dd == 7 else None))
            b.wait("vector", sT)
            b.waitev("vector", xl)
            if t >= 2 and half == 0:
                b.waitev("vector", store_done[par])
            src = pstb.rearrange("p (d k) -> p d k", k=128)
            xs = xrow_v(par).rearrange("p (d k) -> p d k", k=128)[
                :, half * 8:half * 8 + 8, :]
            ds = out_sb[:, par * D:(par + 1) * D].rearrange(
                "p (d k) -> p d k", k=128)[:, half * 8:half * 8 + 8, :]
            out_done[half] = b.op("vector", "tensor_add", ds, src, xs,
                                  sem=sV)
        add_done[t] = out_done[1]
        b.waitev("sync", out_done[1])
        store_done[par] = b.dma("sync", y_out[t * 128:(t + 1) * 128, :],
                                out_sb[:, par * D:(par + 1) * D], sSTp[par])

    return finish()



def host_tables(core):
    t0 = core * TOK
    thetas = 1.0 / 10000.0 ** np.linspace(0.0, 1.0, HD // 2)
    pos = np.arange(t0, t0 + TOK, dtype=np.float64)
    ang = pos[:, None] * thetas[None, :]
    sin = np.sin(ang).astype(np.float32)
    cos = np.cos(ang).astype(np.float32)
    g = 1.0 - np.exp(np.linspace(math.log(1 / 32), math.log(1 / 512), H))
    kt = np.arange(CS, dtype=np.float64)
    dist = kt[None, :] - kt[:, None]
    maskT = np.zeros((H, CS, CS), dtype=np.float32)
    for h in range(H):
        m = np.where(dist >= 0, g[h] ** np.maximum(dist, 0.0) / SCALE, 0.0)
        maskT[h] = m.astype(np.float32)
    maskT = maskT.reshape(H, 2 * 128, CS)
    kd = np.zeros((128, 2 * H), dtype=np.float32)
    qd = np.zeros((128, 2 * H), dtype=np.float32)
    for h in range(H):
        for ktile in range(2):
            idx = np.arange(ktile * 128, ktile * 128 + 128, dtype=np.float64)
            kd[:, h * 2 + ktile] = ((g[h] ** (CS - 1.0 - idx)) / SCALE)
            qd[:, h * 2 + ktile] = (g[h] ** (idx + 1.0))
    coef = np.zeros((128, H * NCORES), dtype=np.float32)
    for h in range(H):
        for s in range(NCORES):
            if s < core:
                coef[:, h * NCORES + s] = np.float32(
                    g[h] ** (1024.0 * (core - 1 - s)))
    return dict(sin=sin, cos=cos, maskT=maskT, kdec=kd, qdec=qd, coef=coef)


def col_layout(v):
    return np.ascontiguousarray(
        np.asarray(v, np.float32).reshape(-1, 128).T).astype(np.float32)




def make_inputs(inputs, core):
    t = host_tables(core)
    f32 = lambda a: np.ascontiguousarray(np.asarray(a, dtype=np.float32))
    gncols = np.concatenate([col_layout(inputs["L0_attn_gn_weight"]),
                             col_layout(inputs["L0_attn_gn_bias"])], 1)
    sccols = np.concatenate([col_layout(inputs["L0_ffn_gate_scale"]),
                             col_layout(inputs["L0_ffn_up_scale"]),
                             col_layout(inputs["L0_ffn_down_scale"])], 1)
    x = f32(inputs["L0_attn_x"]).reshape(NCORES * TOK, D)
    return {
        "x": np.ascontiguousarray(x[core * TOK:(core + 1) * TOK]),
        "wq": f32(inputs["L0_attn_w_q"]), "wk": f32(inputs["L0_attn_w_k"]),
        "wv": f32(inputs["L0_attn_w_v"]), "wg": f32(inputs["L0_attn_w_g"]),
        "wo": f32(inputs["L0_attn_w_o"]),
        "gate_w": f32(inputs["L0_ffn_gate_w"]),
        "up_w": f32(inputs["L0_ffn_up_w"]),
        "down_w": f32(inputs["L0_ffn_down_w"]),
        "biasf": np.stack([f32(inputs["L0_attn_b_q"]),
                           f32(inputs["L0_attn_b_k"]),
                           f32(inputs["L0_attn_b_v"])]),
        "bcols": np.concatenate([col_layout(inputs["L0_attn_b_g"]),
                                 col_layout(inputs["L0_attn_b_o"])], 1),
        "gncols": gncols, "sccols": sccols,
        "sin": t["sin"], "cos": t["cos"], "maskT": t["maskT"],
        "kdec": t["kdec"], "qdec": t["qdec"], "coef": t["coef"],
        "ident_t": np.eye(128, dtype=np.float32),
    }


_CACHE = {}


def kernel(**inputs):
    from concourse.bass_utils import run_bass_kernel_spmd
    if "prog" not in _CACHE:
        _CACHE["prog"] = build_program()
    nc, stack, _ = _CACHE["prog"]
    in_maps = [make_inputs(inputs, c) for c in range(NCORES)]
    res = run_bass_kernel_spmd(nc, in_maps, core_ids=list(range(NCORES)))
    y = np.concatenate([np.asarray(res.results[c]["y"], dtype=np.float32)
                        for c in range(NCORES)], axis=0)
    return y.reshape(1, NCORES * TOK, D)
